# version 10
# baseline (speedup 1.0000x reference)
import numpy as np
import ml_dtypes
from contextlib import ExitStack
import concourse.bass as bass
import concourse.mybir as mybir
from concourse.bass_utils import run_bass_kernel_spmd
from concourse.bass import IndirectOffsetOnAxis

F32 = mybir.dt.float32
BF16 = mybir.dt.bfloat16
I32 = mybir.dt.int32
AF = mybir.ActivationFunctionType
OP = mybir.AluOpType
AX = mybir.AxisListType

D = 2048
T = 2048
HALF = 1024
KT = 16
DT = 16
NST = 16
NFT = 33
E = 32
FH = 768
MT = 6
EPS = 1e-5
CAPN = 384
BIGF = 20000.0
BIGTOK = 2000.0

ENGS = ("sync", "gpsimd", "vector", "scalar", "tensor")


class Rec:
    def __init__(self):
        self.ops = {k: [] for k in ENGS}

    def on(self, eng, meth, *a, **kw):
        rec = [meth, a, kw, None]
        self.ops[eng].append(rec)

        class _H:
            def then_inc(_s, sem, k=1):
                rec[3] = (sem, k)

        return _H()

    def play(self, eng_name, engine):
        for meth, a, kw, inc in self.ops[eng_name]:
            r = getattr(engine, meth)(*a, **kw)
            if inc is not None:
                r.then_inc(inc[0], inc[1])
        self.ops[eng_name] = []

    def play_all(self, nc):
        with nc.Block() as blk:
            @blk.sync
            def _(s):
                self.play("sync", s)

            @blk.gpsimd
            def _(g):
                self.play("gpsimd", g)

            @blk.vector
            def _(v):
                self.play("vector", v)

            @blk.scalar
            def _(sc):
                self.play("scalar", sc)

            @blk.tensor
            def _(t):
                self.play("tensor", t)


class Ctr:
    def __init__(self, sem):
        self.s = sem
        self.n = 0

    def bump(self, k=1):
        self.n += k
        return self.n


def build_nc(debug=False):
    nc = bass.Bass("TRN2", target_bir_lowering=False)

    xb = nc.dram_tensor("xb", [T, D], F32, kind="ExternalInput")
    mrow = nc.dram_tensor("mrow", [1, T], F32, kind="ExternalInput")
    Wm = nc.dram_tensor("Wm", [NFT, 128, KT, 128], F32, kind="ExternalInput")
    bmr = nc.dram_tensor("bmr", [128, NFT], F32, kind="ExternalInput")
    araw = nc.dram_tensor("araw", [128, DT * NST], F32, kind="ExternalInput")
    crw = nc.dram_tensor("crw", [128, DT * NST], F32, kind="ExternalInput")
    cwr = nc.dram_tensor("cwr", [128, DT * 4], F32, kind="ExternalInput")
    bpr = nc.dram_tensor("bpr", [128, 1], F32, kind="ExternalInput")
    selr = nc.dram_tensor("selr", [NST, NST * 128], BF16, kind="ExternalInput")
    identb = nc.dram_tensor("identb", [128, 128], BF16, kind="ExternalInput")
    identf = nc.dram_tensor("identf", [128, 128], F32, kind="ExternalInput")
    wgtr = nc.dram_tensor("wgtr", [128, KT * E], F32, kind="ExternalInput")
    w1h = nc.dram_tensor("w1h", [E, 128, KT, FH], F32, kind="ExternalInput")
    v1h = nc.dram_tensor("v1h", [E, 128, KT, FH], F32, kind="ExternalInput")
    w2h = nc.dram_tensor("w2h", [E, 4, 128, MT, 512], F32, kind="ExternalInput")

    eoffr = nc.dram_tensor("eoffr", [128, 8 * E], F32, kind="ExternalInput")
    tokidr = nc.dram_tensor("tokidr", [128, 8], F32, kind="ExternalInput")
    tbl0r = nc.dram_tensor("tbl0r", [E * CAPN, 2], F32, kind="ExternalInput")

    out = nc.dram_tensor("out", [HALF, D], F32, kind="ExternalOutput")
    if debug:
        dx2 = nc.dram_tensor("dx2", [HALF, D], F32, kind="ExternalOutput")
        dw8 = nc.dram_tensor("dw8", [128, 8 * E], F32, kind="ExternalOutput")
        dtbl = nc.dram_tensor("dtbl", [E * CAPN, 2], F32, kind="ExternalOutput")
        dacc = nc.dram_tensor("dacc", [HALF, D], F32, kind="ExternalOutput")
    o1s = nc.dram_tensor("o1s", [HALF, D], F32, kind="Internal")
    x2s = nc.dram_tensor("x2s", [HALF, D], F32, kind="Internal")
    x2nb = nc.dram_tensor("x2nb", [HALF, D], BF16, kind="Internal")
    tblD = nc.dram_tensor("tblD", [E * CAPN, 2], F32, kind="Internal")
    accH = nc.dram_tensor("accH", [HALF, D], F32, kind="Internal")

    R = Rec()

    with ExitStack() as ctx:
        en = ctx.enter_context

        PE = Ctr(en(nc.semaphore("PE")))
        DV = Ctr(en(nc.semaphore("DV")))
        AC = Ctr(en(nc.semaphore("AC")))
        PH = Ctr(en(nc.semaphore("PH")))
        PO = Ctr(en(nc.semaphore("PO")))
        CONST = Ctr(en(nc.semaphore("CONST")))
        WGT = Ctr(en(nc.semaphore("WGT")))
        SL = [Ctr(en(nc.semaphore(f"SL{i}"))) for i in range(2)]
        OS = [Ctr(en(nc.semaphore(f"OS{i}"))) for i in range(2)]
        O1T = Ctr(en(nc.semaphore("O1T")))
        X2S = Ctr(en(nc.semaphore("X2S")))
        WM = [Ctr(en(nc.semaphore(f"WM{i}"))) for i in range(3)]
        WR = [Ctr(en(nc.semaphore(f"WR{i}"))) for i in range(4)]
        W2R = [Ctr(en(nc.semaphore(f"W2R{i}"))) for i in range(2)]
        TS = [Ctr(en(nc.semaphore(f"TS{i}"))) for i in range(4)]
        SA = Ctr(en(nc.semaphore("SA")))
        GA = [Ctr(en(nc.semaphore(f"GA{i}"))) for i in range(2)]
        PL = [Ctr(en(nc.semaphore(f"PL{i}"))) for i in range(2)]

        pA = en(nc.psum_tensor("pA", [128, 2048], F32))
        pB = en(nc.psum_tensor("pB", [128, 2048], F32))
        pAb = pA.bitcast(BF16)
        pBb = pB.bitcast(BF16)

        A_sb = en(nc.sbuf_tensor("A_sb", [128, DT * NST], F32))
        crow_sb = en(nc.sbuf_tensor("crow_sb", [128, DT * NST], F32))
        cw_sb = en(nc.sbuf_tensor("cw_sb", [128, DT * 4], F32))
        bm_sb = en(nc.sbuf_tensor("bm_sb", [128, NFT], F32))
        bp_sb = en(nc.sbuf_tensor("bp_sb", [128, 1], F32))
        identb_sb = en(nc.sbuf_tensor("identb_sb", [128, 128], BF16))
        identf_sb = en(nc.sbuf_tensor("identf_sb", [128, 128], F32))
        onesf = en(nc.sbuf_tensor("onesf", [1, 128], F32))
        wgt_sb = en(nc.sbuf_tensor("wgt_sb", [128, KT * E], BF16))

        def w(eng, C, val):
            R.on(eng, "wait_ge", C.s, val)

        def gwait(eng, g):
            if g is None:
                return
            if isinstance(g, list):
                for gg in g:
                    gwait(eng, gg)
                return
            R.on(eng, "wait_ge", g[0], g[1])

        def barrier(vec_op, sc_op, pe_op):
            tgt = PH.n + 5
            PH.n = tgt
            R.on("sync", "nop").then_inc(PH.s, 1)
            R.on("gpsimd", "nop").then_inc(PH.s, 1)
            vec_op().then_inc(PH.s, 1)
            sc_op().then_inc(PH.s, 1)
            pe_op().then_inc(PH.s, 1)
            for e2 in ENGS:
                R.on(e2, "wait_ge", PH.s, tgt)

        with ExitStack() as mctx:
            em = mctx.enter_context
            lnT = em(nc.sbuf_tensor("lnT", [128, KT * T], BF16))
            stat = em(nc.sbuf_tensor("stat", [128, 8], F32))
            sx = [em(nc.sbuf_tensor(f"sx{i}", [128, T], BF16)) for i in range(3)]
            swv = [em(nc.sbuf_tensor(f"swv{i}", [128, HALF], BF16)) for i in range(3)]
            wmr = [em(nc.sbuf_tensor(f"wmr{i}", [128, KT * 128], BF16)) for i in range(3)]
            xcd = em(nc.sbuf_tensor("xcd", [128, T], F32))
            dAr = [em(nc.sbuf_tensor(f"dA{i}", [128, T], F32)) for i in range(2)]
            Ur = [em(nc.sbuf_tensor(f"U{i}", [128, T], F32)) for i in range(2)]
            hr = [em(nc.sbuf_tensor(f"h{i}", [128, T], F32)) for i in range(2)]
            xstage = dAr
            lnb = [Ur[0].bitcast(BF16), Ur[1].bitcast(BF16)]
            f0sb = hr[1]
            mrow_sb = hr[0]
            ysb = em(nc.sbuf_tensor("ysb", [128, HALF], F32))
            repsb = [em(nc.sbuf_tensor(f"repsb{i}", [128, T], BF16)) for i in range(2)]
            dreps = em(nc.sbuf_tensor("dreps", [128, T], F32))
            dbr = em(nc.sbuf_tensor("dbr", [NST, T], BF16))
            sel_sb = em(nc.sbuf_tensor("sel_sb", [NST, NST * 128], BF16))
            mrep3 = em(nc.sbuf_tensor("mrep3", [128, 8], BF16))
            o1h = em(nc.sbuf_tensor("o1h", [128, HALF], F32))
            o1t = em(nc.sbuf_tensor("o1t", [128, HALF], F32))

            for dst, src in [
                (mrow_sb[0:1, :], mrow.ap()), (bm_sb[:], bmr.ap()),
                (A_sb[:], araw.ap()), (crow_sb[:], crw.ap()),
                (cw_sb[:], cwr.ap()), (bp_sb[:], bpr.ap()),
                (identb_sb[:], identb.ap()), (identf_sb[:], identf.ap()),
                (sel_sb[:], selr.ap()),
            ]:
                R.on("sync", "dma_start", dst, src).then_inc(CONST.s, 16)
                CONST.bump(16)
            g_consts = (CONST.s, CONST.n)

            R.on("gpsimd", "dma_start", wgt_sb[:], wgtr.ap()).then_inc(WGT.s, 16)
            g_wgt = (WGT.s, WGT.bump(16))

            gwait("scalar", g_consts)
            R.on("scalar", "activation", A_sb[:], A_sb[:], AF.Exp).then_inc(AC.s, 1)
            ac_aexp = AC.bump()
            w("vector", AC, ac_aexp)
            R.on("vector", "tensor_scalar_mul", A_sb[:], A_sb[:], -1.0).then_inc(DV.s, 1)
            dv_aneg = DV.bump()

            gwait("vector", g_consts)
            R.on("vector", "memset", onesf[:], 1.0).then_inc(DV.s, 1)
            dv_ones = DV.bump()
            w("tensor", DV, dv_ones)
            R.on("tensor", "matmul", pB[:, 0:8], onesf[:], mrow_sb[0:1, HALF - 8:HALF],
                 start=True, stop=True).then_inc(PE.s, 1)
            pe_mrep = PE.bump()
            w("vector", PE, pe_mrep)
            R.on("vector", "tensor_copy", mrep3[:], pB[:, 0:8]).then_inc(DV.s, 1)
            dv_mrep = DV.bump()

            paguard = [None] * 4
            pa_i = [0]

            def pa_next():
                i = pa_i[0] % 4
                pa_i[0] += 1
                return i

            xst_guard = [None, None]
            lnb_guard = [None, None]
            prev_var_dv = None
            ac_sq_last = [0]
            for tt in range(KT):
                sl = tt % 2
                gwait("sync", xst_guard[sl])
                R.on("sync", "dma_start", xstage[sl][:],
                     xb.ap()[tt * 128:(tt + 1) * 128, :]).then_inc(SL[sl].s, 16)
                g_x = (SL[sl].s, SL[sl].bump(16))
                gwait("scalar", g_x)
                if prev_var_dv is not None:
                    w("scalar", DV, prev_var_dv)
                if tt == 0:
                    w("scalar", DV, dv_mrep)
                R.on("scalar", "drain")
                R.on("scalar", "activation", pB[:], xstage[sl][:], AF.Square,
                     accum_out=stat[:, 1:2]).then_inc(AC.s, 1)
                ac_sq = AC.bump()
                ac_sq_last[0] = ac_sq
                gwait("vector", g_x)
                R.on("vector", "tensor_reduce", stat[:, 0:1], xstage[sl][:], AX.X, OP.add)
                R.on("vector", "drain")
                R.on("vector", "tensor_scalar_mul", stat[:, 2:3], stat[:, 0:1], -1.0 / D)
                w("vector", AC, ac_sq)
                R.on("vector", "tensor_scalar", stat[:, 3:4], stat[:, 1:2], 1.0 / D,
                     float(EPS), OP.mult, OP.add)
                R.on("vector", "drain")
                R.on("vector", "tensor_tensor", stat[:, 4:5], stat[:, 2:3], stat[:, 2:3], OP.mult)
                R.on("vector", "drain")
                R.on("vector", "tensor_tensor", stat[:, 4:5], stat[:, 3:4], stat[:, 4:5],
                     OP.subtract).then_inc(DV.s, 1)
                dv_var = DV.bump()
                prev_var_dv = dv_var
                w("scalar", DV, dv_var)
                R.on("scalar", "drain")
                R.on("scalar", "activation", stat[:, 5:6], stat[:, 4:5],
                     AF.Sqrt).then_inc(AC.s, 1)
                ac_sd = AC.bump()
                w("vector", AC, ac_sd)
                R.on("vector", "reciprocal", stat[:, 6:7], stat[:, 5:6])
                R.on("vector", "drain")
                gwait("vector", lnb_guard[sl])
                R.on("vector", "tensor_scalar", lnb[sl][:, 0:D], xstage[sl][:], stat[:, 2:3],
                     stat[:, 6:7], OP.add, OP.mult).then_inc(DV.s, 1)
                dv_ln = DV.bump()
                xst_guard[sl] = [(DV.s, dv_ln), (AC.s, ac_sq)]
                pe_last = None
                for q in range(2):
                    bi = pa_next()
                    gwait("tensor", paguard[bi])
                    w("tensor", DV, dv_ln)
                    for i in range(8):
                        j = q * 8 + i
                        h = R.on("tensor", "transpose",
                             pAb[:, bi * 1024 + i * 128: bi * 1024 + (i + 1) * 128],
                             lnb[sl][:, j * 128:(j + 1) * 128], identb_sb[:])
                    h.then_inc(PE.s, 1)
                    pe_tr = PE.bump()
                    pe_last = pe_tr
                    w("vector", PE, pe_tr)
                    R.on("vector", "tensor_copy",
                         lnT[:].rearrange("p (j t) -> p j t", t=T)
                            [:, q * 8:(q + 1) * 8, tt * 128:(tt + 1) * 128],
                         pAb[:, bi * 1024:(bi + 1) * 1024]
                            .rearrange("p (i c) -> p i c", c=128)).then_inc(DV.s, 1)
                    dv_ev = DV.bump()
                    paguard[bi] = (DV.s, dv_ev)
                lnb_guard[sl] = (PE.s, pe_last)
            dv_lnT_done = DV.n

            wm_guard = [None] * 3
            strip_arrive = [None] * NFT

            def emit_strip_dma(s):
                sl = s % 3
                gwait("gpsimd", wm_guard[sl])
                R.on("gpsimd", "dma_start",
                     wmr[sl][:].rearrange("p (k c) -> p k c", c=128),
                     Wm.ap()[s]).then_inc(WM[sl].s, 16)
                strip_arrive[s] = (WM[sl].s, WM[sl].bump(16))

            for s0 in range(3):
                emit_strip_dma(s0)

            state = dict(
                sx_guard=[None] * 3, swv_guard=[None] * 3,
                rows_dv=None, dreps_dv=None,
                pb_guard=(DV.s, dv_mrep),
                dA_guard=[None, None],
                rep_guard=[None, None],
                U_guard=[None, None],
                ysb_guard=None, o1h_guard=None, o1t_guard=None,
                scan_done={}, strip_pe={},
            )

            def emit_strip(s):
                th_list = (0, 1, 2, 3) if (s == 0 or s % 2 == 1) else (2, 3)
                last_ac = None
                for th in th_list:
                    bi = pa_next()
                    gwait("tensor", paguard[bi])
                    gwait("tensor", strip_arrive[s])
                    if s == 0 and th == 0:
                        w("tensor", DV, dv_lnT_done)
                    for k in range(KT):
                        h = R.on("tensor", "matmul", pA[:, bi * 512:(bi + 1) * 512],
                             wmr[s % 3][:, k * 128:(k + 1) * 128],
                             lnT[:, k * T + th * 512: k * T + th * 512 + 512],
                             start=(k == 0), stop=(k == KT - 1))
                    h.then_inc(PE.s, 1)
                    pe_mm = PE.bump()
                    state["strip_pe"][s] = pe_mm
                    w("scalar", PE, pe_mm)
                    psl = pA[:, bi * 512:(bi + 1) * 512]
                    if s == 0:
                        R.on("scalar", "activation", f0sb[:, th * 512:(th + 1) * 512],
                             psl, AF.Identity, bias=bm_sb[:, s:s + 1]).then_inc(AC.s, 1)
                    elif s % 2 == 1:
                        j = (s - 1) // 2
                        if th == th_list[0]:
                            gwait("scalar", state["sx_guard"][j % 3])
                        R.on("scalar", "activation", sx[j % 3][:, th * 512:(th + 1) * 512],
                             psl, AF.Silu, bias=bm_sb[:, s:s + 1]).then_inc(AC.s, 1)
                    else:
                        j = s // 2 - 1
                        if th == th_list[0]:
                            gwait("scalar", state["swv_guard"][j % 3])
                        R.on("scalar", "activation",
                             swv[j % 3][:, (th - 2) * 512:(th - 1) * 512],
                             psl, AF.Silu, bias=bm_sb[:, s:s + 1]).then_inc(AC.s, 1)
                    last_ac = AC.bump()
                    paguard[bi] = (AC.s, last_ac)
                return last_ac

            def emit_f0rows(ac_f0):
                r1 = xcd[0:1, :]
                r2 = Ur[0][0:1, :]
                w("scalar", AC, ac_f0)
                R.on("scalar", "drain")
                R.on("scalar", "activation", r1, f0sb[64:65, :], AF.Abs)
                R.on("scalar", "drain")
                R.on("scalar", "activation", r2, r1, AF.Exp, scale=-1.0)
                R.on("scalar", "drain")
                R.on("scalar", "activation", r1, r2, AF.Ln, bias=1.0).then_inc(AC.s, 1)
                ac_ln = AC.bump()
                w("vector", AC, ac_ln)
                R.on("vector", "tensor_scalar", r2, f0sb[64:65, :], 0.0, None, OP.max)
                R.on("vector", "drain")
                R.on("vector", "tensor_tensor", r2, r1, r2, OP.add).then_inc(DV.s, 1)
                dv_sp = DV.bump()
                w("scalar", DV, dv_sp)
                R.on("scalar", "drain")
                R.on("scalar", "activation", r1, r2, AF.Silu).then_inc(AC.s, 1)
                ac_si = AC.bump()
                w("vector", AC, ac_si)
                gwait("vector", g_consts)
                R.on("vector", "tensor_tensor", r2, r1, mrow_sb[0:1, :], OP.mult).then_inc(DV.s, 1)
                dv_dm = DV.bump()
                gwait("tensor", state["pb_guard"])
                w("tensor", AC, ac_sq_last[0])
                w("tensor", DV, dv_dm)
                for q in range(4):
                    h = R.on("tensor", "matmul", pB[:, q * 512:(q + 1) * 512], onesf[:],
                         r2[:, q * 512:(q + 1) * 512], start=True, stop=True)
                h.then_inc(PE.s, 1)
                pe_dr = PE.bump()
                w("vector", PE, pe_dr)
                R.on("vector", "tensor_copy", dreps[:], pB[:]).then_inc(DV.s, 1)
                state["dreps_dv"] = DV.bump()
                state["pb_guard"] = (DV.s, state["dreps_dv"])
                R.on("vector", "tensor_scalar", f0sb[0:16, :], f0sb[0:16, :],
                     bp_sb[0:16, :], None, OP.add)
                R.on("vector", "drain")
                R.on("vector", "tensor_tensor", dbr[:], dreps[0:16, :], f0sb[0:16, :],
                     OP.mult).then_inc(DV.s, 1)
                state["rows_dv"] = DV.bump()

            def emit_scanblock(j, ac_sx):
                sl = j % 3
                w("vector", AC, ac_sx)
                w("vector", DV, state["rows_dv"])
                w("vector", DV, dv_mrep)
                gwait("vector", state.get("xcd_po"))
                R.on("vector", "tensor_tensor", sx[sl][:, HALF - 3:HALF],
                     sx[sl][:, HALF - 3:HALF], mrep3[:, 5:8], OP.mult)
                R.on("vector", "drain")
                R.on("vector", "tensor_scalar", xcd[:], sx[sl][:],
                     cw_sb[:, j * 4 + 3:j * 4 + 4], None, OP.mult)
                R.on("vector", "drain")
                for kk in (2, 1, 0):
                    off = 3 - kk
                    h = R.on("vector", "scalar_tensor_tensor", xcd[:, off:T],
                             sx[sl][:, 0:T - off], cw_sb[:, j * 4 + kk:j * 4 + kk + 1],
                             xcd[:, off:T], OP.mult, OP.add)
                    R.on("vector", "drain")
                h.then_inc(DV.s, 1)
                dv_xcd2 = DV.bump()
                state["sx_guard"][sl] = (DV.s, dv_xcd2)
                w("gpsimd", DV, dv_xcd2)

                pe_rep = [None] * NST
                ac_dA = [None] * NST
                dv_scan = [None] * NST
                dv_y = [None] * NST
                po_u = [None] * NST
                for step in range(NST + 2):
                    n = step
                    if n < NST:
                        gwait("tensor", state["pb_guard"])
                        if j == 0 and n == 0:
                            w("tensor", DV, state["rows_dv"])
                        for q in range(4):
                            h = R.on("tensor", "matmul", pB[:, q * 512:(q + 1) * 512],
                                 sel_sb[:, n * 128:(n + 1) * 128],
                                 dbr[:, q * 512:(q + 1) * 512],
                                 start=True, stop=True)
                        h.then_inc(PE.s, 1)
                        pe_rep[n] = PE.bump()
                        w("scalar", PE, pe_rep[n])
                        gwait("scalar", state["rep_guard"][n % 2])
                        R.on("scalar", "activation", repsb[n % 2][:], pB[:],
                             AF.Identity).then_inc(AC.s, 1)
                        ac_rep = AC.bump()
                        state["pb_guard"] = (AC.s, ac_rep)
                        gwait("scalar", state["dA_guard"][n % 2])
                        if j == 0 and n == 0:
                            w("scalar", DV, state["dreps_dv"])
                            w("scalar", DV, dv_aneg)
                        R.on("scalar", "activation", dAr[n % 2][:], dreps[:], AF.Exp,
                             scale=A_sb[:, j * NST + n: j * NST + n + 1]).then_inc(AC.s, 1)
                        ac_dA[n] = AC.bump()
                        w("gpsimd", AC, ac_rep)
                        gwait("gpsimd", state["U_guard"][n % 2])
                        R.on("gpsimd", "tensor_tensor", Ur[n % 2][:], xcd[:],
                             repsb[n % 2][:], OP.mult).then_inc(PO.s, 1)
                        po_u[n] = PO.bump()
                        R.on("gpsimd", "drain")
                        state["rep_guard"][n % 2] = (PO.s, po_u[n])
                    m = step - 1
                    if 0 <= m < NST:
                        w("vector", AC, ac_dA[m])
                        w("vector", PO, po_u[m])
                        R.on("vector", "tensor_tensor_scan", hr[m % 2][:], dAr[m % 2][:],
                             Ur[m % 2][:], 0.0, OP.mult, OP.add).then_inc(DV.s, 1)
                        dv_scan[m] = DV.bump()
                        R.on("vector", "drain")
                        state["dA_guard"][m % 2] = (DV.s, dv_scan[m])
                        state["U_guard"][m % 2] = (DV.s, dv_scan[m])
                    p2 = step - 2
                    if 0 <= p2 < NST:
                        if p2 == 0:
                            gwait("vector", state["ysb_guard"])
                            h = R.on("vector", "tensor_scalar", ysb[:], hr[p2 % 2][:, HALF:T],
                                 crow_sb[:, j * NST + p2: j * NST + p2 + 1], None, OP.mult)
                        else:
                            h = R.on("vector", "scalar_tensor_tensor", ysb[:],
                                 hr[p2 % 2][:, HALF:T],
                                 crow_sb[:, j * NST + p2: j * NST + p2 + 1], ysb[:],
                                 OP.mult, OP.add)
                        h.then_inc(DV.s, 1)
                        dv_y[p2] = DV.bump()
                        R.on("vector", "drain")
                state["scan_done"][j] = dv_y[NST - 1]
                state["xcd_po"] = (PO.s, po_u[NST - 1])

            def emit_gate(j, ac_v):
                sl = j % 3
                w("vector", AC, ac_v)
                w("vector", DV, state["scan_done"][j])
                gwait("vector", state["o1h_guard"])
                R.on("vector", "drain")
                R.on("vector", "tensor_tensor", o1h[:], ysb[:], swv[sl][:],
                     OP.mult).then_inc(DV.s, 1)
                dv_gate = DV.bump()
                state["ysb_guard"] = (DV.s, dv_gate)
                state["swv_guard"][sl] = (DV.s, dv_gate)
                dv_last = None
                for q in range(2):
                    bi = pa_next()
                    gwait("tensor", paguard[bi])
                    w("tensor", DV, dv_gate)
                    for i in range(4):
                        ttk = q * 4 + i
                        h = R.on("tensor", "transpose",
                             pA[:, bi * 512 + i * 128: bi * 512 + (i + 1) * 128],
                             o1h[:, ttk * 128:(ttk + 1) * 128], identf_sb[:])
                    h.then_inc(PE.s, 1)
                    pe_t = PE.bump()
                    if q == 0:
                        gwait("vector", state["o1t_guard"])
                    w("vector", PE, pe_t)
                    R.on("vector", "tensor_copy", o1t[:, q * 512:(q + 1) * 512],
                         pA[:, bi * 512:(bi + 1) * 512]).then_inc(DV.s, 1)
                    dv_last = DV.bump()
                    paguard[bi] = (DV.s, dv_last)
                state["o1h_guard"] = (DV.s, dv_last)
                w("sync", DV, dv_last)
                R.on("sync", "dma_start",
                     o1s.ap()[:, j * 128:(j + 1) * 128].rearrange("(tt p) c -> p tt c", p=128),
                     o1t[:].rearrange("p (tt c) -> p tt c", c=128)).then_inc(O1T.s, 16)
                state["o1t_guard"] = (O1T.s, O1T.bump(16))

            for s in range(NFT):
                ac_last = emit_strip(s)
                wm_guard[s % 3] = (PE.s, state["strip_pe"][s])
                if s + 3 < NFT:
                    emit_strip_dma(s + 3)
                if s == 0:
                    emit_f0rows(ac_last)
                elif s % 2 == 1:
                    emit_scanblock((s - 1) // 2, ac_last)
                else:
                    emit_gate(s // 2 - 1, ac_last)
            g_o1_all = (O1T.s, O1T.n)

            bi_b = pa_next()
            gb_b = paguard[bi_b]
            def _pe_bar():
                gwait("tensor", gb_b)
                return R.on("tensor", "transpose",
                            pAb[:, bi_b * 1024: bi_b * 1024 + 128],
                            identb_sb[:], identb_sb[:])
            barrier(
                lambda: R.on("vector", "tensor_copy", stat[:, 7:8], stat[:, 0:1]),
                lambda: R.on("scalar", "activation", stat[:, 6:7], stat[:, 0:1],
                             AF.Identity),
                _pe_bar,
            )
            R.play_all(nc)

        with ExitStack() as fctx:
            ef = fctx.enter_context
            x2nT = ef(nc.sbuf_tensor("x2nT", [128, KT * HALF], BF16))
            sg = ef(nc.sbuf_tensor("sg", [128, MT * CAPN], BF16))
            hti = ef(nc.sbuf_tensor("hti", [128, MT * CAPN], BF16))
            wr = [ef(nc.sbuf_tensor(f"wr{i}", [128, 4 * FH], BF16)) for i in range(4)]
            w2r = [ef(nc.sbuf_tensor(f"w2r{i}", [128, MT * 512], BF16)) for i in range(2)]
            xsg = [ef(nc.sbuf_tensor(f"xsg_{c}", [128, D], BF16)) for c in range(3)]
            xsT = ef(nc.sbuf_tensor("xsT", [128, KT * CAPN], BF16))
            ysb2 = [ef(nc.sbuf_tensor(f"ysb2{i}", [128, D], F32)) for i in range(6)]
            prow = [ef(nc.sbuf_tensor(f"prow{i}", [128, 8], F32)) for i in range(2)]
            idxI3 = [ef(nc.sbuf_tensor(f"idxI3{i}", [128, 4], I32)) for i in range(2)]
            W8T = ef(nc.sbuf_tensor("W8T", [32, 1024], F32))
            maskT = ef(nc.sbuf_tensor("maskT", [32, 1024], F32))
            onesT = ef(nc.sbuf_tensor("onesT", [32, 1024], F32))
            posTok = ef(nc.sbuf_tensor("posTok", [128, 8 * E], F32))
            eoff_sb = ef(nc.sbuf_tensor("eoff_sb", [128, 8 * E], F32))
            tokid_sb = ef(nc.sbuf_tensor("tokid_sb", [128, 8], F32))
            vrec = ef(nc.sbuf_tensor("vrec", [128, 8 * E * 2], F32))
            idxI = ef(nc.sbuf_tensor("idxI", [128, 8 * E], I32))
            Lsb = ef(nc.sbuf_tensor("Lsb", [128, 8 * E], F32))
            Wk = ef(nc.sbuf_tensor("Wk", [128, 8 * E], F32))
            Wwork = ef(nc.sbuf_tensor("Wwork", [128, 8 * E], F32))
            Wtmp = ef(nc.sbuf_tensor("Wtmp", [128, 8 * E], F32))
            Wsmall = ef(nc.sbuf_tensor("Wsmall", [128, 8], F32))
            Wrm = ef(nc.sbuf_tensor("Wrm", [128, 8], F32))
            xs2 = [ef(nc.sbuf_tensor(f"xs2{i}", [128, D], F32)) for i in range(2)]
            _os2 = ef(nc.sbuf_tensor("os2s", [128, D], F32))
            os2 = [_os2, _os2]
            _lnb2 = ef(nc.sbuf_tensor("lnb2s", [128, D], BF16))
            lnb2 = [_lnb2, _lnb2]
            stat2 = ef(nc.sbuf_tensor("stat2", [128, 8], F32))

            paguard2 = [None] * 4
            pa2_i = [0]

            def pa2_next():
                i = pa2_i[0] % 4
                pa2_i[0] += 1
                return i

            xs_guard = [None, None]
            os_guard = [None, None]
            lnb2_guard = [None, None]
            lnb2_st_guard = [None, None]
            prev_var_dv2 = None
            ac_sq2_last = [0]
            for tt in range(8):
                sl = tt % 2
                gwait("sync", xs_guard[sl])
                R.on("sync", "dma_start", xs2[sl][:],
                     xb.ap()[HALF + tt * 128: HALF + (tt + 1) * 128, :]).then_inc(SL[sl].s, 16)
                g_xin = (SL[sl].s, SL[sl].bump(16))
                gwait("sync", os_guard[sl])
                gwait("sync", g_o1_all)
                R.on("sync", "dma_start", os2[sl][:],
                     o1s.ap()[tt * 128:(tt + 1) * 128, :]).then_inc(OS[sl].s, 16)
                g_oin = (OS[sl].s, OS[sl].bump(16))
                gwait("vector", g_xin)
                gwait("vector", g_oin)
                R.on("vector", "tensor_tensor", xs2[sl][:], xs2[sl][:], os2[sl][:],
                     OP.add).then_inc(DV.s, 1)
                dv_x2 = DV.bump()
                R.on("vector", "drain")
                os_guard[0] = os_guard[1] = (DV.s, dv_x2)
                w("sync", DV, dv_x2)
                R.on("sync", "dma_start", x2s.ap()[tt * 128:(tt + 1) * 128, :],
                     xs2[sl][:]).then_inc(SL[sl].s, 16)
                SL[sl].bump(16)
                if debug:
                    R.on("sync", "dma_start", dx2.ap()[tt * 128:(tt + 1) * 128, :],
                         xs2[sl][:]).then_inc(SL[sl].s, 16)
                    SL[sl].bump(16)
                g_x2w = (SL[sl].s, SL[sl].n)
                w("scalar", DV, dv_x2)
                if prev_var_dv2 is not None:
                    w("scalar", DV, prev_var_dv2)
                R.on("scalar", "drain")
                R.on("scalar", "activation", pB[:], xs2[sl][:], AF.Square,
                     accum_out=stat2[:, 1:2]).then_inc(AC.s, 1)
                ac_sq = AC.bump()
                ac_sq2_last[0] = ac_sq
                R.on("vector", "tensor_reduce", stat2[:, 0:1], xs2[sl][:], AX.X, OP.add)
                R.on("vector", "drain")
                R.on("vector", "tensor_scalar_mul", stat2[:, 2:3], stat2[:, 0:1], -1.0 / D)
                w("vector", AC, ac_sq)
                R.on("vector", "tensor_scalar", stat2[:, 3:4], stat2[:, 1:2], 1.0 / D,
                     float(EPS), OP.mult, OP.add)
                R.on("vector", "drain")
                R.on("vector", "tensor_tensor", stat2[:, 4:5], stat2[:, 2:3], stat2[:, 2:3], OP.mult)
                R.on("vector", "drain")
                R.on("vector", "tensor_tensor", stat2[:, 4:5], stat2[:, 3:4], stat2[:, 4:5],
                     OP.subtract).then_inc(DV.s, 1)
                dv_var = DV.bump()
                prev_var_dv2 = dv_var
                w("scalar", DV, dv_var)
                R.on("scalar", "drain")
                R.on("scalar", "activation", stat2[:, 5:6], stat2[:, 4:5],
                     AF.Sqrt).then_inc(AC.s, 1)
                ac_sd = AC.bump()
                w("vector", AC, ac_sd)
                R.on("vector", "reciprocal", stat2[:, 6:7], stat2[:, 5:6])
                R.on("vector", "drain")
                gwait("vector", lnb2_guard[sl])
                gwait("vector", lnb2_st_guard[sl])
                R.on("vector", "tensor_scalar", lnb2[sl][:], xs2[sl][:], stat2[:, 2:3],
                     stat2[:, 6:7], OP.add, OP.mult).then_inc(DV.s, 1)
                dv_ln = DV.bump()
                w("sync", DV, dv_ln)
                R.on("sync", "dma_start", x2nb.ap()[tt * 128:(tt + 1) * 128, :],
                     lnb2[sl][:]).then_inc(SL[sl].s, 16)
                lnb2_st_guard[0] = lnb2_st_guard[1] = (SL[sl].s, SL[sl].bump(16))
                xs_guard[sl] = [(DV.s, dv_ln), (AC.s, ac_sq), (SL[sl].s, SL[sl].n)]
                pe_last = None
                for q in range(2):
                    bi = pa2_next()
                    gwait("tensor", paguard2[bi])
                    w("tensor", DV, dv_ln)
                    for i in range(8):
                        j = q * 8 + i
                        h = R.on("tensor", "transpose",
                             pAb[:, bi * 1024 + i * 128: bi * 1024 + (i + 1) * 128],
                             lnb2[sl][:, j * 128:(j + 1) * 128], identb_sb[:])
                    h.then_inc(PE.s, 1)
                    pe_t = PE.bump()
                    pe_last = pe_t
                    w("vector", PE, pe_t)
                    R.on("vector", "tensor_copy",
                         x2nT[:].rearrange("p (j t) -> p j t", t=HALF)
                             [:, q * 8:(q + 1) * 8, tt * 128:(tt + 1) * 128],
                         pAb[:, bi * 1024:(bi + 1) * 1024]
                             .rearrange("p (i c) -> p i c", c=128)).then_inc(DV.s, 1)
                    dv_e = DV.bump()
                    paguard2[bi] = (DV.s, dv_e)
                lnb2_guard[0] = lnb2_guard[1] = (PE.s, pe_last)
            dv_x2nT = DV.n

            lg_guard = None
            for g8 in range(8):
                w("tensor", DV, dv_x2nT)
                w("tensor", AC, ac_sq2_last[0])
                gwait("tensor", g_wgt)
                gwait("tensor", lg_guard)
                for k in range(KT):
                    h = R.on("tensor", "matmul", pB[:, 1536:1536 + E],
                         x2nT[:, k * HALF + g8 * 128: k * HALF + (g8 + 1) * 128],
                         wgt_sb[:, k * E:(k + 1) * E],
                         start=(k == 0), stop=(k == KT - 1))
                h.then_inc(PE.s, 1)
                pe_l = PE.bump()
                w("vector", PE, pe_l)
                R.on("vector", "tensor_copy", Lsb[:, g8 * E:(g8 + 1) * E],
                     pB[:, 1536:1536 + E]).then_inc(DV.s, 1)
                lg_guard = (DV.s, DV.bump())
            L3 = Lsb[:].rearrange("p (g e) -> p g e", e=E)
            W3 = Wk[:].rearrange("p (g e) -> p g e", e=E)
            wk3 = Wwork[:].rearrange("p (g e) -> p g e", e=E)
            tq3 = Wtmp[:].rearrange("p (g e) -> p g e", e=E)
            gwait("vector", lg_guard)
            R.on("vector", "tensor_copy", Wwork[:], Lsb[:])
            R.on("vector", "memset", Wk[:], 0.0)
            R.on("vector", "drain")
            for it in range(8):
                R.on("vector", "tensor_reduce", Wsmall[:], wk3, AX.X, OP.max)
                R.on("vector", "drain")
                if it == 0:
                    R.on("vector", "tensor_copy", Wrm[:], Wsmall[:])
                    R.on("vector", "drain")
                R.on("vector", "tensor_tensor", tq3, wk3,
                     Wsmall[:].broadcast_to([128, 8, E]), OP.is_equal)
                R.on("vector", "drain")
                R.on("vector", "tensor_tensor", W3, W3, tq3, OP.max)
                R.on("vector", "scalar_tensor_tensor", wk3, tq3, -1e30, wk3,
                     OP.mult, OP.add)
                R.on("vector", "drain")
            R.on("vector", "tensor_tensor", tq3, L3,
                 Wrm[:].broadcast_to([128, 8, E]), OP.subtract).then_inc(DV.s, 1)
            dv_sub = DV.bump()
            w("scalar", DV, dv_sub)
            R.on("scalar", "drain")
            R.on("scalar", "activation", Wtmp[:], Wtmp[:], AF.Exp).then_inc(AC.s, 1)
            ac_exp = AC.bump()
            w("vector", AC, ac_exp)
            R.on("vector", "tensor_tensor", W3, tq3, W3, OP.mult)
            R.on("vector", "drain")
            R.on("vector", "tensor_reduce", Wsmall[:], W3, AX.X, OP.add)
            R.on("vector", "drain")
            R.on("vector", "reciprocal", Wrm[:], Wsmall[:])
            R.on("vector", "drain")
            R.on("vector", "tensor_tensor", W3, W3,
                 Wrm[:].broadcast_to([128, 8, E]), OP.mult).then_inc(DV.s, 1)
            dv_w8 = DV.bump()
            if debug:
                w("sync", DV, dv_w8)
                R.on("sync", "dma_start", dw8.ap(), Wk[:]).then_inc(X2S.s, 16)
                X2S.bump(16)

            rb_tok = nc.gpsimd.alloc_register()
            rb_tbl = nc.gpsimd.alloc_register()
            R.on("gpsimd", "reg_mov", rb_tok, HALF - 1)
            R.on("gpsimd", "reg_mov", rb_tbl, E * CAPN - 1)
            for dst2, src2 in [(eoff_sb[:], eoffr.ap()), (tokid_sb[:], tokidr.ap())]:
                R.on("sync", "dma_start", dst2, src2).then_inc(CONST.s, 16)
                CONST.bump(16)
            R.on("sync", "dma_start", tblD.ap(), tbl0r.ap()).then_inc(CONST.s, 16)
            g_tbl0 = (CONST.s, CONST.bump(16))
            g_consts2 = (CONST.s, CONST.n)

            R.on("vector", "memset", ysb2[0][:], 0.0).then_inc(DV.s, 1)
            dv_z = DV.bump()
            w("sync", DV, dv_z)
            for tt in range(8):
                R.on("sync", "dma_start", accH.ap()[tt * 128:(tt + 1) * 128, :],
                     ysb2[0][:]).then_inc(X2S.s, 16)
                X2S.bump(16)
            g_accz = (X2S.s, X2S.n)

            bk3_guard = {"A": None, "B": None}
            for q in range(2):
                key = "A" if q == 0 else "B"
                pt3 = pA if q == 0 else pB
                gwait("tensor", bk3_guard[key])
                w("tensor", DV, dv_w8)
                for i in range(4):
                    g8i = q * 4 + i
                    h = R.on("tensor", "transpose",
                             pt3[0:32, 1536 + i * 128:1536 + (i + 1) * 128],
                             Wk[:, g8i * 32:(g8i + 1) * 32], identf_sb[:])
                h.then_inc(PE.s, 1)
                pe_t = PE.bump()
                w("vector", PE, pe_t)
                R.on("vector", "tensor_copy",
                     W8T[:, q * 512:(q + 1) * 512].rearrange("p (i c) -> p i c", c=128),
                     pt3[0:32, 1536:2048].rearrange("p (i c) -> p i c", c=128)).then_inc(DV.s, 1)
                bk3_guard[key] = (DV.s, DV.bump())
            gwait("vector", bk3_guard["B"])
            R.on("vector", "memset", onesT[:], 1.0)
            R.on("vector", "tensor_scalar", maskT[:], W8T[:], 0.0, None, OP.is_gt)
            R.on("vector", "drain")
            R.on("vector", "tensor_tensor_scan", W8T[:], onesT[:], maskT[:], 0.0,
                 OP.mult, OP.add).then_inc(DV.s, 1)
            dv_pos = DV.bump()
            R.on("vector", "drain")
            gwait("tensor", bk3_guard["A"])
            w("tensor", DV, dv_pos)
            for g8i in range(8):
                h = R.on("tensor", "transpose",
                         pA[:, 1536 + g8i * 32:1536 + (g8i + 1) * 32],
                         W8T[:, g8i * 128:(g8i + 1) * 128], identf_sb[0:32, 0:32])
            h.then_inc(PE.s, 1)
            pe_pt = PE.bump()
            w("vector", PE, pe_pt)
            R.on("vector", "tensor_copy", posTok[:], pA[:, 1536:1536 + 256]).then_inc(DV.s, 1)
            dv_ptok = DV.bump()
            bk3_guard["A"] = (DV.s, dv_ptok)
            R.on("vector", "drain")
            gwait("vector", g_consts2)
            R.on("vector", "tensor_tensor", Wwork[:], posTok[:], eoff_sb[:], OP.add)
            R.on("vector", "drain")
            R.on("vector", "tensor_scalar", Wtmp[:], Wk[:], 0.0, None, OP.is_le)
            R.on("vector", "drain")
            R.on("vector", "scalar_tensor_tensor", Wwork[:], Wtmp[:], BIGF, Wwork[:],
                 OP.mult, OP.add)
            R.on("vector", "drain")
            R.on("vector", "tensor_scalar", Wtmp[:], posTok[:], float(CAPN), None, OP.is_gt)
            R.on("vector", "drain")
            R.on("vector", "scalar_tensor_tensor", Wwork[:], Wtmp[:], BIGF, Wwork[:],
                 OP.mult, OP.add)
            R.on("vector", "drain")
            R.on("vector", "tensor_copy", idxI[:], Wwork[:])
            R.on("vector", "drain")
            v4 = vrec[:].rearrange("p (g e two) -> p g e two", e=E, two=2)
            R.on("vector", "tensor_copy", v4[:, :, :, 0:1],
                 tokid_sb[:].rearrange("p g -> p g () ()").broadcast_to([128, 8, E, 1])
                 if False else tokid_sb[:].broadcast_to([128, 8, E, 1]))
            R.on("vector", "tensor_copy", v4[:, :, :, 1:2],
                 Wk[:].rearrange("p (g e two) -> p g e two", e=E, two=1)).then_inc(DV.s, 1)
            dv_vrec = DV.bump()
            R.on("vector", "drain")

            i4 = idxI[:].rearrange("p (g e) -> p g e", e=E)
            ts_done = [None] * E

            def emit_scatters(e):
                r = e % 4
                w("gpsimd", DV, dv_vrec)
                gwait("gpsimd", g_tbl0)
                for g8i in range(8):
                    R.on("gpsimd", "indirect_dma_start",
                         tblD.ap(), IndirectOffsetOnAxis(i4[:, g8i, e:e + 1], 0),
                         v4[:, g8i, e, :], None,
                         bounds_check=rb_tbl, oob_is_err=False).then_inc(TS[r].s, 16)
                    TS[r].bump(16)
                ts_done[e] = (TS[r].s, TS[r].n)

            for e0 in range(4):
                emit_scatters(e0)

            wr_guard = [None] * 4
            w2_guard = [None, None]
            w2_arr = [None, None]
            wrc = [0]

            def load_wchunks(mat, e):
                arr = []
                for c in range(4):
                    slid = wrc[0] % 4
                    wrc[0] += 1
                    gwait("gpsimd", wr_guard[slid])
                    R.on("gpsimd", "dma_start",
                         wr[slid][:].rearrange("p (c m) -> p c m", m=FH),
                         mat.ap()[e, :, c * 4:(c + 1) * 4, :]).then_inc(WR[slid].s, 16)
                    arr.append((slid, (WR[slid].s, WR[slid].bump(16))))
                return arr

            def gb(mt):
                return (pA if mt < 3 else pB)[:, (mt % 3) * 512:(mt % 3) * 512 + CAPN]

            g_x2nb = [(SL[0].s, SL[0].n), (SL[1].s, SL[1].n)]
            sg_guard = None
            hti_guard = None
            prow_guard = [None, None]
            idx_guard = [None, None]
            xsg_guard = None
            xsT_guard = None
            ysb_guard = [g_accz] + [None] * 5
            gu_bank_guard = None
            first_g = [True]

            def load_stage(e):
                slp = e % 2
                gwait("sync", ts_done[e])
                gwait("sync", prow_guard[slp])
                for ct in range(3):
                    R.on("sync", "dma_start", prow[slp][:, ct * 2:ct * 2 + 2],
                         tblD.ap()[e * CAPN + ct * 128: e * CAPN + (ct + 1) * 128, :]
                         ).then_inc(PL[slp].s, 16)
                    PL[slp].bump(16)
                g_pl = (PL[slp].s, PL[slp].n)
                gwait("vector", g_pl)
                gwait("vector", idx_guard[slp])
                R.on("vector", "tensor_copy", idxI3[slp][:, 0:3],
                     prow[slp][:].rearrange("p (c two) -> p c two", two=2)[:, 0:3, 0]
                     ).then_inc(DV.s, 1)
                dv_idx = DV.bump()
                R.on("vector", "drain")
                w("gpsimd", DV, dv_idx)
                gwait("gpsimd", xsg_guard)
                gwait("gpsimd", g_x2nb[0])
                gwait("gpsimd", g_x2nb[1])
                for ct in range(3):
                    R.on("gpsimd", "indirect_dma_start",
                         xsg[ct][:], None,
                         x2nb.ap(), IndirectOffsetOnAxis(idxI3[slp][:, ct:ct + 1], 0),
                         bounds_check=rb_tok, oob_is_err=False).then_inc(GA[e % 2].s, 16)
                    GA[e % 2].bump(16)
                g_ga = (GA[e % 2].s, GA[e % 2].n)
                w1a = load_wchunks(w1h, e)
                w2pre = [None, None]
                for dc in range(2):
                    slid = dc % 2
                    gwait("gpsimd", w2_guard[slid])
                    R.on("gpsimd", "dma_start",
                         w2r[slid][:].rearrange("p (k c) -> p k c", c=512),
                         w2h.ap()[e, dc]).then_inc(W2R[slid].s, 16)
                    w2pre[dc] = (W2R[slid].s, W2R[slid].bump(16))
                return dict(slp=slp, g_ga=g_ga, w1a=w1a, w2pre=w2pre)

            def compute_stage(e, LD):
                nonlocal sg_guard, hti_guard, xsg_guard, xsT_guard, gu_bank_guard
                slp = LD["slp"]
                g_ga = LD["g_ga"]
                dv_tev = None
                for ct in range(3):
                    for q in range(2):
                        key = "A" if (ct * 2 + q) % 2 == 0 else "B"
                        ptb = pAb if key == "A" else pBb
                        gwait("tensor", bk3_guard[key])
                        gwait("tensor", g_ga)
                        for i in range(8):
                            k = q * 8 + i
                            h = R.on("tensor", "transpose",
                                     ptb[:, 3072 + i * 128:3072 + (i + 1) * 128],
                                     xsg[ct][:, k * 128:(k + 1) * 128],
                                     identb_sb[:])
                        h.then_inc(PE.s, 1)
                        pe_t = PE.bump()
                        w("vector", PE, pe_t)
                        if ct == 0 and q == 0:
                            gwait("vector", xsT_guard)
                        dst = xsT[:].rearrange("p (k c) -> p k c", c=CAPN)[
                            :, q * 8:(q + 1) * 8, ct * 128:(ct + 1) * 128]
                        R.on("vector", "tensor_copy",
                             dst,
                             ptb[:, 3072:4096].rearrange("p (i c) -> p i c", c=128)
                             ).then_inc(DV.s, 1)
                        dv_tev = DV.bump()
                        bk3_guard[key] = (DV.s, dv_tev)
                dv_xsT = dv_tev
                xsg_guard = (DV.s, dv_xsT)

                w1a = LD["w1a"]
                gwait("tensor", gu_bank_guard)
                w("tensor", DV, dv_xsT)
                if first_g[0]:
                    gwait("tensor", lg_guard)
                    first_g[0] = False
                pe_gc = []
                for k in range(KT):
                    slid, arr = w1a[k // 4]
                    if k % 4 == 0:
                        gwait("tensor", arr)
                    for mt in range(MT):
                        h = R.on("tensor", "matmul", gb(mt),
                                 wr[slid][:, (k % 4) * FH + mt * 128:(k % 4) * FH + (mt + 1) * 128],
                                 xsT[:, k * CAPN:(k + 1) * CAPN],
                                 start=(k == 0), stop=(k == KT - 1))
                        if mt == MT - 1 and k % 4 == 3:
                            h.then_inc(PE.s, 1)
                    if k % 4 == 3:
                        pe_gc.append(PE.bump())
                for c in range(4):
                    wr_guard[w1a[c][0]] = (PE.s, pe_gc[c])
                pe_g = pe_gc[-1]
                gwait("scalar", sg_guard)
                w("scalar", PE, pe_g)
                for mt in range(MT):
                    h = R.on("scalar", "activation", sg[:, mt * CAPN:(mt + 1) * CAPN],
                             gb(mt), AF.Silu)
                h.then_inc(AC.s, 1)
                ac_sg = AC.bump()
                v1a = load_wchunks(v1h, e)
                w("tensor", AC, ac_sg)
                pe_uc = []
                for k in range(KT):
                    slid, arr = v1a[k // 4]
                    if k % 4 == 0:
                        gwait("tensor", arr)
                    for mt in range(MT):
                        h = R.on("tensor", "matmul", gb(mt),
                                 wr[slid][:, (k % 4) * FH + mt * 128:(k % 4) * FH + (mt + 1) * 128],
                                 xsT[:, k * CAPN:(k + 1) * CAPN],
                                 start=(k == 0), stop=(k == KT - 1))
                        if mt == MT - 1 and k % 4 == 3:
                            h.then_inc(PE.s, 1)
                    if k % 4 == 3:
                        pe_uc.append(PE.bump())
                for c in range(4):
                    wr_guard[v1a[c][0]] = (PE.s, pe_uc[c])
                pe_u = pe_uc[-1]
                gwait("vector", hti_guard)
                w("vector", PE, pe_u)
                w("vector", AC, ac_sg)
                for mt in range(MT):
                    h = R.on("vector", "tensor_tensor", hti[:, mt * CAPN:(mt + 1) * CAPN],
                             sg[:, mt * CAPN:(mt + 1) * CAPN], gb(mt), OP.mult)
                h.then_inc(DV.s, 1)
                dv_h = DV.bump()
                R.on("vector", "drain")
                sg_guard = (DV.s, dv_h)
                gu_bank_guard = (DV.s, dv_h)
                xsT_guard = (PE.s, pe_u)

                dv_ye = None
                pe_y = None
                for dc in range(4):
                    slid = dc % 2
                    if dc < 2:
                        w2_arr[slid] = LD["w2pre"][dc]
                    else:
                        gwait("gpsimd", w2_guard[slid])
                        R.on("gpsimd", "dma_start",
                             w2r[slid][:].rearrange("p (k c) -> p k c", c=512),
                             w2h.ap()[e, dc]).then_inc(W2R[slid].s, 16)
                        w2_arr[slid] = (W2R[slid].s, W2R[slid].bump(16))
                    for ct in range(3):
                        key = "A" if (dc * 3 + ct) % 2 == 0 else "B"
                        pt3 = pA if key == "A" else pB
                        w("tensor", DV, dv_h)
                        gwait("tensor", w2_arr[slid])
                        gwait("tensor", bk3_guard[key])
                        for k in range(MT):
                            h = R.on("tensor", "matmul", pt3[:, 1536:2048],
                                     hti[:, k * CAPN + ct * 128: k * CAPN + (ct + 1) * 128],
                                     w2r[slid][:, k * 512:(k + 1) * 512],
                                     start=(k == 0), stop=(k == MT - 1))
                        h.then_inc(PE.s, 1)
                        pe_y = PE.bump()
                        if ct == 2:
                            w2_guard[slid] = (PE.s, pe_y)
                        w("vector", PE, pe_y)
                        if dc == 0:
                            gwait("vector", ysb_guard[slp * 3 + ct])
                        h = R.on("vector", "tensor_scalar",
                                 ysb2[slp * 3 + ct][:, dc * 512:(dc + 1) * 512],
                                 pt3[:, 1536:2048],
                                 prow[slp][:, ct * 2 + 1:ct * 2 + 2], None, OP.mult)
                        h.then_inc(DV.s, 1)
                        dv_ye = DV.bump()
                        R.on("vector", "drain")
                        bk3_guard[key] = (DV.s, dv_ye)
                prow_guard[slp] = (DV.s, dv_ye)
                hti_guard = (PE.s, pe_y)
                return dv_ye

            def scatter_stage(e, dv_ye_last, g_ga_e):
                slp = e % 2
                w("gpsimd", DV, dv_ye_last)
                gwait("gpsimd", g_accz)
                gwait("gpsimd", (SA.s, SA.n))
                for ct in range(3):
                    for hh in range(2):
                        R.on("gpsimd", "indirect_dma_start",
                             accH.ap(),
                             IndirectOffsetOnAxis(idxI3[slp][:, ct:ct + 1], 0),
                             ysb2[slp * 3 + ct][:, hh * 1024:(hh + 1) * 1024], None,
                             element_offset=hh * 1024,
                             bounds_check=rb_tok, oob_is_err=False,
                             compute_op=OP.add).then_inc(SA.s, 16)
                        SA.bump(16)
                for ct in range(3):
                    ysb_guard[slp * 3 + ct] = (SA.s, SA.n)
                idx_guard[slp] = [g_ga_e, (SA.s, SA.n)]

            LD = load_stage(0)
            for e in range(E):
                g_ga_e = LD["g_ga"]
                dv_ye_last = compute_stage(e, LD)
                if e + 1 < E:
                    LD = load_stage(e + 1)
                    if e + 4 < E:
                        emit_scatters(e + 4)
                scatter_stage(e, dv_ye_last, g_ga_e)

            g_sa_all = (SA.s, SA.n)
            if debug:
                for e2 in range(E):
                    gwait("sync", ts_done[e2])
                R.on("sync", "dma_start", dtbl.ap(), tblD.ap()).then_inc(X2S.s, 16)
                X2S.bump(16)
                gwait("sync", g_sa_all)
                for tt in range(8):
                    R.on("sync", "dma_start", dacc.ap()[tt * 128:(tt + 1) * 128, :],
                         accH.ap()[tt * 128:(tt + 1) * 128, :]).then_inc(X2S.s, 16)
                    X2S.bump(16)

            for tt in range(8):
                sl = tt % 2
                gwait("sync", xs_guard[sl])
                R.on("sync", "dma_start", xs2[sl][:],
                     x2s.ap()[tt * 128:(tt + 1) * 128, :]).then_inc(SL[sl].s, 16)
                g_r = (SL[sl].s, SL[sl].bump(16))
                gwait("sync", os_guard[sl])
                gwait("sync", g_sa_all)
                R.on("sync", "dma_start", os2[sl][:],
                     accH.ap()[tt * 128:(tt + 1) * 128, :]).then_inc(OS[sl].s, 16)
                g_a = (OS[sl].s, OS[sl].bump(16))
                gwait("vector", g_r)
                gwait("vector", g_a)
                R.on("vector", "tensor_tensor", xs2[sl][:], xs2[sl][:], os2[sl][:],
                     OP.add).then_inc(DV.s, 1)
                dv_o = DV.bump()
                R.on("vector", "drain")
                os_guard[0] = os_guard[1] = (DV.s, dv_o)
                w("sync", DV, dv_o)
                R.on("sync", "dma_start",
                     out.ap()[tt * 128:(tt + 1) * 128, :],
                     xs2[sl][:]).then_inc(SL[sl].s, 16)
                xs_guard[sl] = [(SL[sl].s, SL[sl].bump(16))]

            for Cc in [CONST, SL[0], SL[1], OS[0], OS[1], O1T, X2S] + PL:
                R.on("sync", "wait_ge", Cc.s, Cc.n)
            for Cc in [WGT, SA] + WM + WR + W2R + TS + GA:
                R.on("gpsimd", "wait_ge", Cc.s, Cc.n)

            R.play_all(nc)

    return nc


def _host_prep(inputs):
    f32 = np.float32
    x = np.asarray(inputs["L0_attn_x"], f32)
    W = np.asarray(inputs["L0_attn_W_in"], f32)
    b = np.asarray(inputs["L0_attn_b_in"], f32)
    A_log = np.asarray(inputs["L0_attn_A_log"], f32)
    B_param = np.asarray(inputs["L0_attn_B_param"], f32)
    C_param = np.asarray(inputs["L0_attn_C_param"], f32)
    conv_w = np.asarray(inputs["L0_attn_conv_w"], f32)
    wg = np.asarray(inputs["L0_ffn_wg"], f32)
    w1 = np.asarray(inputs["L0_ffn_w1"], f32)
    v1 = np.asarray(inputs["L0_ffn_v1"], f32)
    w2 = np.asarray(inputs["L0_ffn_w2"], f32)

    W_cat = np.zeros((D, NFT * 128), f32)
    b_cat = np.zeros(NFT * 128, f32)
    W_cat[:, 0:16] = W[:, 2 * D:2 * D + 16]
    b_cat[0:16] = b[2 * D:2 * D + 16]
    W_cat[:, 64] = W[:, 2 * D + 16]
    b_cat[64] = b[2 * D + 16]
    for j in range(16):
        W_cat[:, (2 * j + 1) * 128:(2 * j + 2) * 128] = W[:, D + j * 128:D + (j + 1) * 128]
        b_cat[(2 * j + 1) * 128:(2 * j + 2) * 128] = b[D + j * 128:D + (j + 1) * 128]
        W_cat[:, (2 * j + 2) * 128:(2 * j + 3) * 128] = W[:, j * 128:(j + 1) * 128]
        b_cat[(2 * j + 2) * 128:(2 * j + 3) * 128] = b[j * 128:(j + 1) * 128]
    Wmh = np.ascontiguousarray(W_cat.reshape(KT, 128, NFT, 128).transpose(2, 1, 0, 3))
    bmh = np.ascontiguousarray(b_cat.reshape(NFT, 128).T)

    arw = np.ascontiguousarray(A_log.reshape(DT, 128, NST).transpose(1, 0, 2).reshape(128, DT * NST))
    crh = np.ascontiguousarray(C_param.reshape(DT, 128, NST).transpose(1, 0, 2).reshape(128, DT * NST))
    cwh = np.ascontiguousarray(conv_w[:, 0, :].reshape(DT, 128, 4).transpose(1, 0, 2).reshape(128, DT * 4))
    bp128 = np.zeros((128, 1), f32)
    bp128[0:16, 0] = B_param
    selm = np.zeros((NST, NST * 128), ml_dtypes.bfloat16)
    for n in range(NST):
        selm[n, n * 128:(n + 1) * 128] = 1.0
    idb = np.eye(128, dtype=ml_dtypes.bfloat16)
    idf = np.eye(128, dtype=f32)
    wgth = np.ascontiguousarray(wg.T.reshape(KT, 128, E).transpose(1, 0, 2).reshape(128, KT * E))
    w1hh = np.ascontiguousarray(w1.reshape(E, KT, 128, FH).transpose(0, 2, 1, 3))
    v1hh = np.ascontiguousarray(v1.reshape(E, KT, 128, FH).transpose(0, 2, 1, 3))
    w2hh = np.ascontiguousarray(w2.reshape(E, MT, 128, 4, 512).transpose(0, 3, 2, 1, 4))

    eoffm = np.zeros((128, 8 * E), f32)
    for e in range(E):
        eoffm[:, e::E] = e * CAPN - 1
    tokidm = np.zeros((128, 8), f32)
    for g in range(8):
        tokidm[:, g] = g * 128 + np.arange(128)
    tbl0 = np.zeros((E * CAPN, 2), f32)
    tbl0[:, 0] = BIGTOK

    shared = dict(Wm=Wmh, bmr=bmh, araw=arw, crw=crh, cwr=cwh, bpr=bp128,
                  selr=selm, identb=idb, identf=idf, wgtr=wgth,
                  w1h=w1hh, v1h=v1hh, w2h=w2hh,
                  eoffr=eoffm, tokidr=tokidm, tbl0r=tbl0)

    in_maps = []
    for bidx in range(4):
        for half in range(2):
            if half == 0:
                xbuf = np.concatenate([np.zeros((HALF, D), f32), x[bidx, 0:HALF]], axis=0)
                mr = np.concatenate([np.zeros((1, HALF), f32), np.ones((1, HALF), f32)], axis=1)
            else:
                xbuf = np.ascontiguousarray(x[bidx])
                mr = np.ones((1, T), f32)
            m = dict(shared)
            m["xb"] = xbuf
            m["mrow"] = mr
            in_maps.append(m)
    return in_maps


def kernel(**inputs):
    nc = build_nc(debug=False)
    in_maps = _host_prep(inputs)
    res = run_bass_kernel_spmd(nc, in_maps, core_ids=list(range(8)))
    outs = [res.results[c]["out"] for c in range(8)]
    full = np.concatenate(outs, axis=0).reshape(4, T, D)
    return full.astype(np.float32)


# version 12 (final)
# speedup vs baseline: 1.1100x; 1.1100x over previous
import numpy as np
import ml_dtypes
from contextlib import ExitStack
import concourse.bass as bass
import concourse.mybir as mybir
from concourse.bass_utils import run_bass_kernel_spmd
from concourse.bass import IndirectOffsetOnAxis

F32 = mybir.dt.float32
BF16 = mybir.dt.bfloat16
I32 = mybir.dt.int32
AF = mybir.ActivationFunctionType
OP = mybir.AluOpType
AX = mybir.AxisListType

D = 2048
T = 2048
HALF = 1024
KT = 16
DT = 16
NST = 16
NFT = 33
E = 32
FH = 768
MT = 6
EPS = 1e-5
CAPN = 384
BIGF = 20000.0
BIGTOK = 2000.0

ENGS = ("sync", "gpsimd", "vector", "scalar", "tensor")


class Rec:
    def __init__(self):
        self.ops = {k: [] for k in ENGS}

    def on(self, eng, meth, *a, **kw):
        rec = [meth, a, kw, None]
        self.ops[eng].append(rec)

        class _H:
            def then_inc(_s, sem, k=1):
                rec[3] = (sem, k)

        return _H()

    def play(self, eng_name, engine):
        for meth, a, kw, inc in self.ops[eng_name]:
            r = getattr(engine, meth)(*a, **kw)
            if inc is not None:
                r.then_inc(inc[0], inc[1])
        self.ops[eng_name] = []

    def play_all(self, nc):
        with nc.Block() as blk:
            @blk.sync
            def _(s):
                self.play("sync", s)

            @blk.gpsimd
            def _(g):
                self.play("gpsimd", g)

            @blk.vector
            def _(v):
                self.play("vector", v)

            @blk.scalar
            def _(sc):
                self.play("scalar", sc)

            @blk.tensor
            def _(t):
                self.play("tensor", t)


class Ctr:
    def __init__(self, sem):
        self.s = sem
        self.n = 0

    def bump(self, k=1):
        self.n += k
        return self.n


def build_nc(debug=False):
    nc = bass.Bass("TRN2", target_bir_lowering=False)

    xb = nc.dram_tensor("xb", [T, D], F32, kind="ExternalInput")
    mrow = nc.dram_tensor("mrow", [1, T], F32, kind="ExternalInput")
    Wm = nc.dram_tensor("Wm", [NFT, 128, KT, 128], F32, kind="ExternalInput")
    bmr = nc.dram_tensor("bmr", [128, NFT], F32, kind="ExternalInput")
    araw = nc.dram_tensor("araw", [128, DT * NST], F32, kind="ExternalInput")
    crw = nc.dram_tensor("crw", [128, DT * NST], F32, kind="ExternalInput")
    cwr = nc.dram_tensor("cwr", [128, DT * 4], F32, kind="ExternalInput")
    bpr = nc.dram_tensor("bpr", [128, 1], F32, kind="ExternalInput")
    selr = nc.dram_tensor("selr", [NST, NST * 128], BF16, kind="ExternalInput")
    identb = nc.dram_tensor("identb", [128, 128], BF16, kind="ExternalInput")
    identf = nc.dram_tensor("identf", [128, 128], F32, kind="ExternalInput")
    wgtr = nc.dram_tensor("wgtr", [128, KT * E], F32, kind="ExternalInput")
    w1h = nc.dram_tensor("w1h", [E, 128, KT, FH], F32, kind="ExternalInput")
    v1h = nc.dram_tensor("v1h", [E, 128, KT, FH], F32, kind="ExternalInput")
    w2h = nc.dram_tensor("w2h", [E, 4, 128, MT, 512], F32, kind="ExternalInput")

    eoffr = nc.dram_tensor("eoffr", [128, 8 * E], F32, kind="ExternalInput")
    tokidr = nc.dram_tensor("tokidr", [128, 8], F32, kind="ExternalInput")
    tbl0r = nc.dram_tensor("tbl0r", [E * CAPN, 2], F32, kind="ExternalInput")

    out = nc.dram_tensor("out", [HALF, D], F32, kind="ExternalOutput")
    if debug:
        dx2 = nc.dram_tensor("dx2", [HALF, D], F32, kind="ExternalOutput")
        dw8 = nc.dram_tensor("dw8", [128, 8 * E], F32, kind="ExternalOutput")
        dtbl = nc.dram_tensor("dtbl", [E * CAPN, 2], F32, kind="ExternalOutput")
        dacc = nc.dram_tensor("dacc", [HALF, D], F32, kind="ExternalOutput")
    o1s = nc.dram_tensor("o1s", [HALF, D], F32, kind="Internal")
    x2s = nc.dram_tensor("x2s", [HALF, D], F32, kind="Internal")
    x2nb = nc.dram_tensor("x2nb", [HALF, D], BF16, kind="Internal")
    tblD = nc.dram_tensor("tblD", [E * CAPN, 2], F32, kind="Internal")
    accH = nc.dram_tensor("accH", [HALF, D], F32, kind="Internal")

    R = Rec()

    with ExitStack() as ctx:
        en = ctx.enter_context

        PE = Ctr(en(nc.semaphore("PE")))
        DV = Ctr(en(nc.semaphore("DV")))
        AC = Ctr(en(nc.semaphore("AC")))
        PH = Ctr(en(nc.semaphore("PH")))
        CONST = Ctr(en(nc.semaphore("CONST")))
        WGT = Ctr(en(nc.semaphore("WGT")))
        SL = [Ctr(en(nc.semaphore(f"SL{i}"))) for i in range(2)]
        OS = [Ctr(en(nc.semaphore(f"OS{i}"))) for i in range(2)]
        O1T = Ctr(en(nc.semaphore("O1T")))
        X2S = Ctr(en(nc.semaphore("X2S")))
        WM = [Ctr(en(nc.semaphore(f"WM{i}"))) for i in range(4)]
        WR = [Ctr(en(nc.semaphore(f"WR{i}"))) for i in range(4)]
        W2R = [Ctr(en(nc.semaphore(f"W2R{i}"))) for i in range(2)]
        TS = [Ctr(en(nc.semaphore(f"TS{i}"))) for i in range(4)]
        SA = Ctr(en(nc.semaphore("SA")))
        GA = [Ctr(en(nc.semaphore(f"GA{i}"))) for i in range(2)]
        PL = [Ctr(en(nc.semaphore(f"PL{i}"))) for i in range(2)]

        pA = en(nc.psum_tensor("pA", [128, 2048], F32))
        pB = en(nc.psum_tensor("pB", [128, 2048], F32))
        pAb = pA.bitcast(BF16)
        pBb = pB.bitcast(BF16)

        A_sb = en(nc.sbuf_tensor("A_sb", [128, DT * NST], F32))
        crow_sb = en(nc.sbuf_tensor("crow_sb", [128, DT * NST], F32))
        cw_sb = en(nc.sbuf_tensor("cw_sb", [128, DT * 4], F32))
        bm_sb = en(nc.sbuf_tensor("bm_sb", [128, NFT], F32))
        bp_sb = en(nc.sbuf_tensor("bp_sb", [128, 1], F32))
        identb_sb = en(nc.sbuf_tensor("identb_sb", [128, 128], BF16))
        identf_sb = en(nc.sbuf_tensor("identf_sb", [128, 128], F32))
        onesf = en(nc.sbuf_tensor("onesf", [1, 128], F32))
        wgt_sb = en(nc.sbuf_tensor("wgt_sb", [128, KT * E], BF16))

        def w(eng, C, val):
            R.on(eng, "wait_ge", C.s, val)

        def gwait(eng, g):
            if g is None:
                return
            if isinstance(g, list):
                for gg in g:
                    gwait(eng, gg)
                return
            R.on(eng, "wait_ge", g[0], g[1])

        def barrier(vec_op, sc_op, pe_op):
            tgt = PH.n + 5
            PH.n = tgt
            R.on("sync", "nop").then_inc(PH.s, 1)
            R.on("gpsimd", "nop").then_inc(PH.s, 1)
            vec_op().then_inc(PH.s, 1)
            sc_op().then_inc(PH.s, 1)
            pe_op().then_inc(PH.s, 1)
            for e2 in ENGS:
                R.on(e2, "wait_ge", PH.s, tgt)

        with ExitStack() as mctx:
            em = mctx.enter_context
            lnT = em(nc.sbuf_tensor("lnT", [128, KT * T], BF16))
            stat = em(nc.sbuf_tensor("stat", [128, 8], F32))
            sx = [em(nc.sbuf_tensor(f"sx{i}", [128, T], BF16)) for i in range(4)]
            swv = [em(nc.sbuf_tensor(f"swv{i}", [128, HALF], BF16)) for i in range(4)]
            wmr = [em(nc.sbuf_tensor(f"wmr{i}", [128, KT * 128], BF16)) for i in range(4)]
            xcd = em(nc.sbuf_tensor("xcd", [128, T], BF16))
            repsb = [em(nc.sbuf_tensor(f"repsb{i}", [128, T], BF16)) for i in range(2)]
            dAr = [em(nc.sbuf_tensor(f"dA{i}", [128, T], F32)) for i in range(2)]
            Ur = [em(nc.sbuf_tensor(f"U{i}", [128, T], BF16)) for i in range(2)]
            hr = [em(nc.sbuf_tensor(f"h{i}", [128, T], F32)) for i in range(2)]
            xstage = dAr
            lnb = [Ur[0], Ur[1]]
            f0sb = hr[1]
            mrow_sb = hr[0]
            ysb = em(nc.sbuf_tensor("ysb", [128, HALF], F32))
            dreps = em(nc.sbuf_tensor("dreps", [128, T], F32))
            dbr = em(nc.sbuf_tensor("dbr", [NST, T], BF16))
            sel_sb = em(nc.sbuf_tensor("sel_sb", [NST, NST * 128], BF16))
            mrep3 = em(nc.sbuf_tensor("mrep3", [128, 8], BF16))
            o1h = em(nc.sbuf_tensor("o1h", [128, HALF], F32))
            o1t = em(nc.sbuf_tensor("o1t", [128, HALF], F32))

            for dst, src in [
                (mrow_sb[0:1, :], mrow.ap()), (bm_sb[:], bmr.ap()),
                (A_sb[:], araw.ap()), (crow_sb[:], crw.ap()),
                (cw_sb[:], cwr.ap()), (bp_sb[:], bpr.ap()),
                (identb_sb[:], identb.ap()), (identf_sb[:], identf.ap()),
                (sel_sb[:], selr.ap()),
            ]:
                R.on("sync", "dma_start", dst, src).then_inc(CONST.s, 16)
                CONST.bump(16)
            g_consts = (CONST.s, CONST.n)

            R.on("gpsimd", "dma_start", wgt_sb[:], wgtr.ap()).then_inc(WGT.s, 16)
            g_wgt = (WGT.s, WGT.bump(16))

            gwait("scalar", g_consts)
            R.on("scalar", "activation", A_sb[:], A_sb[:], AF.Exp).then_inc(AC.s, 1)
            ac_aexp = AC.bump()
            w("vector", AC, ac_aexp)
            R.on("vector", "tensor_scalar_mul", A_sb[:], A_sb[:], -1.0).then_inc(DV.s, 1)
            dv_aneg = DV.bump()

            gwait("vector", g_consts)
            R.on("vector", "memset", onesf[:], 1.0).then_inc(DV.s, 1)
            dv_ones = DV.bump()
            w("tensor", DV, dv_ones)
            R.on("tensor", "matmul", pB[:, 0:8], onesf[:], mrow_sb[0:1, HALF - 8:HALF],
                 start=True, stop=True).then_inc(PE.s, 1)
            pe_mrep = PE.bump()
            w("vector", PE, pe_mrep)
            R.on("vector", "tensor_copy", mrep3[:], pB[:, 0:8]).then_inc(DV.s, 1)
            dv_mrep = DV.bump()

            paguard = [None] * 4
            pa_i = [0]

            def pa_next():
                i = pa_i[0] % 4
                pa_i[0] += 1
                return i

            xst_guard = [None, None]
            lnb_guard = [None, None]
            prev_var_dv = None
            ac_sq_last = [0]
            for tt in range(KT):
                sl = tt % 2
                gwait("sync", xst_guard[sl])
                R.on("sync", "dma_start", xstage[sl][:],
                     xb.ap()[tt * 128:(tt + 1) * 128, :]).then_inc(SL[sl].s, 16)
                g_x = (SL[sl].s, SL[sl].bump(16))
                gwait("scalar", g_x)
                if prev_var_dv is not None:
                    w("scalar", DV, prev_var_dv)
                if tt == 0:
                    w("scalar", DV, dv_mrep)
                R.on("scalar", "drain")
                R.on("scalar", "activation", pB[:], xstage[sl][:], AF.Square,
                     accum_out=stat[:, 1:2]).then_inc(AC.s, 1)
                ac_sq = AC.bump()
                ac_sq_last[0] = ac_sq
                gwait("vector", g_x)
                R.on("vector", "tensor_reduce", stat[:, 0:1], xstage[sl][:], AX.X, OP.add)
                R.on("vector", "drain")
                R.on("vector", "tensor_scalar_mul", stat[:, 2:3], stat[:, 0:1], -1.0 / D)
                w("vector", AC, ac_sq)
                R.on("vector", "tensor_scalar", stat[:, 3:4], stat[:, 1:2], 1.0 / D,
                     float(EPS), OP.mult, OP.add)
                R.on("vector", "drain")
                R.on("vector", "tensor_tensor", stat[:, 4:5], stat[:, 2:3], stat[:, 2:3], OP.mult)
                R.on("vector", "drain")
                R.on("vector", "tensor_tensor", stat[:, 4:5], stat[:, 3:4], stat[:, 4:5],
                     OP.subtract).then_inc(DV.s, 1)
                dv_var = DV.bump()
                prev_var_dv = dv_var
                w("scalar", DV, dv_var)
                R.on("scalar", "drain")
                R.on("scalar", "activation", stat[:, 5:6], stat[:, 4:5],
                     AF.Sqrt).then_inc(AC.s, 1)
                ac_sd = AC.bump()
                w("vector", AC, ac_sd)
                R.on("vector", "reciprocal", stat[:, 6:7], stat[:, 5:6])
                R.on("vector", "drain")
                gwait("vector", lnb_guard[sl])
                R.on("vector", "tensor_scalar", lnb[sl][:, 0:D], xstage[sl][:], stat[:, 2:3],
                     stat[:, 6:7], OP.add, OP.mult).then_inc(DV.s, 1)
                dv_ln = DV.bump()
                xst_guard[sl] = [(DV.s, dv_ln), (AC.s, ac_sq)]
                pe_last = None
                for q in range(2):
                    bi = pa_next()
                    gwait("tensor", paguard[bi])
                    w("tensor", DV, dv_ln)
                    for i in range(8):
                        j = q * 8 + i
                        h = R.on("tensor", "transpose",
                             pAb[:, bi * 1024 + i * 128: bi * 1024 + (i + 1) * 128],
                             lnb[sl][:, j * 128:(j + 1) * 128], identb_sb[:])
                    h.then_inc(PE.s, 1)
                    pe_tr = PE.bump()
                    pe_last = pe_tr
                    w("vector", PE, pe_tr)
                    R.on("vector", "tensor_copy",
                         lnT[:].rearrange("p (j t) -> p j t", t=T)
                            [:, q * 8:(q + 1) * 8, tt * 128:(tt + 1) * 128],
                         pAb[:, bi * 1024:(bi + 1) * 1024]
                            .rearrange("p (i c) -> p i c", c=128)).then_inc(DV.s, 1)
                    dv_ev = DV.bump()
                    paguard[bi] = (DV.s, dv_ev)
                lnb_guard[sl] = (PE.s, pe_last)
            dv_lnT_done = DV.n

            wm_guard = [None] * 4
            strip_arrive = [None] * NFT

            def emit_strip_dma(s):
                sl = s % 4
                gwait("gpsimd", wm_guard[sl])
                R.on("gpsimd", "dma_start",
                     wmr[sl][:].rearrange("p (k c) -> p k c", c=128),
                     Wm.ap()[s]).then_inc(WM[sl].s, 16)
                strip_arrive[s] = (WM[sl].s, WM[sl].bump(16))

            for s0 in range(4):
                emit_strip_dma(s0)

            state = dict(
                sx_guard=[None] * 4, swv_guard=[None] * 4,
                rows_dv=None, dreps_dv=None,
                pb_guard=(DV.s, dv_mrep),
                dA_guard=[None, None],
                rep_guard=[None, None],
                ysb_guard=None, o1h_guard=None, o1t_guard=None,
                scan_done={}, strip_pe={},
            )

            def emit_strip(s):
                th_list = (0, 1, 2, 3) if (s == 0 or s % 2 == 1) else (2, 3)
                last_ac = None
                for th in th_list:
                    bi = pa_next()
                    gwait("tensor", paguard[bi])
                    gwait("tensor", strip_arrive[s])
                    if s == 0 and th == 0:
                        w("tensor", DV, dv_lnT_done)
                    for k in range(KT):
                        h = R.on("tensor", "matmul", pA[:, bi * 512:(bi + 1) * 512],
                             wmr[s % 4][:, k * 128:(k + 1) * 128],
                             lnT[:, k * T + th * 512: k * T + th * 512 + 512],
                             start=(k == 0), stop=(k == KT - 1))
                    h.then_inc(PE.s, 1)
                    pe_mm = PE.bump()
                    state["strip_pe"][s] = pe_mm
                    w("scalar", PE, pe_mm)
                    psl = pA[:, bi * 512:(bi + 1) * 512]
                    if s == 0:
                        R.on("scalar", "activation", f0sb[:, th * 512:(th + 1) * 512],
                             psl, AF.Identity, bias=bm_sb[:, s:s + 1]).then_inc(AC.s, 1)
                    elif s % 2 == 1:
                        j = (s - 1) // 2
                        if th == th_list[0]:
                            gwait("scalar", state["sx_guard"][j % 4])
                        R.on("scalar", "activation", sx[j % 4][:, th * 512:(th + 1) * 512],
                             psl, AF.Silu, bias=bm_sb[:, s:s + 1]).then_inc(AC.s, 1)
                    else:
                        j = s // 2 - 1
                        if th == th_list[0]:
                            gwait("scalar", state["swv_guard"][j % 4])
                        R.on("scalar", "activation",
                             swv[j % 4][:, (th - 2) * 512:(th - 1) * 512],
                             psl, AF.Silu, bias=bm_sb[:, s:s + 1]).then_inc(AC.s, 1)
                    last_ac = AC.bump()
                    paguard[bi] = (AC.s, last_ac)
                return last_ac

            def emit_f0rows(ac_f0):
                r1 = dreps[0:1, :]
                r2 = hr[0][1:2, :] if False else None
                r2 = dAr[0][0:1, :]
                w("scalar", AC, ac_f0)
                R.on("scalar", "drain")
                R.on("scalar", "activation", r1, f0sb[64:65, :], AF.Abs)
                R.on("scalar", "drain")
                R.on("scalar", "activation", r2, r1, AF.Exp, scale=-1.0)
                R.on("scalar", "drain")
                R.on("scalar", "activation", r1, r2, AF.Ln, bias=1.0).then_inc(AC.s, 1)
                ac_ln = AC.bump()
                w("vector", AC, ac_ln)
                R.on("vector", "tensor_scalar", r2, f0sb[64:65, :], 0.0, None, OP.max)
                R.on("vector", "drain")
                R.on("vector", "tensor_tensor", r2, r1, r2, OP.add).then_inc(DV.s, 1)
                dv_sp = DV.bump()
                w("scalar", DV, dv_sp)
                R.on("scalar", "drain")
                R.on("scalar", "activation", r1, r2, AF.Silu).then_inc(AC.s, 1)
                ac_si = AC.bump()
                w("vector", AC, ac_si)
                gwait("vector", g_consts)
                R.on("vector", "tensor_tensor", r2, r1, mrow_sb[0:1, :], OP.mult).then_inc(DV.s, 1)
                dv_dm = DV.bump()
                gwait("tensor", state["pb_guard"])
                w("tensor", AC, ac_sq_last[0])
                w("tensor", DV, dv_dm)
                for q in range(4):
                    h = R.on("tensor", "matmul", pB[:, q * 512:(q + 1) * 512], onesf[:],
                         r2[:, q * 512:(q + 1) * 512], start=True, stop=True)
                h.then_inc(PE.s, 1)
                pe_dr = PE.bump()
                w("vector", PE, pe_dr)
                R.on("vector", "tensor_copy", dreps[:], pB[:]).then_inc(DV.s, 1)
                state["dreps_dv"] = DV.bump()
                state["pb_guard"] = (DV.s, state["dreps_dv"])
                R.on("vector", "tensor_scalar", f0sb[0:16, :], f0sb[0:16, :],
                     bp_sb[0:16, :], None, OP.add)
                R.on("vector", "drain")
                R.on("vector", "tensor_tensor", dbr[:], dreps[0:16, :], f0sb[0:16, :],
                     OP.mult).then_inc(DV.s, 1)
                state["rows_dv"] = DV.bump()

            def emit_scanblock(j, ac_sx):
                sl = j % 4
                w("vector", AC, ac_sx)
                w("vector", DV, state["rows_dv"])
                w("vector", DV, dv_mrep)
                R.on("vector", "tensor_tensor", sx[sl][:, HALF - 3:HALF],
                     sx[sl][:, HALF - 3:HALF], mrep3[:, 5:8], OP.mult)
                R.on("vector", "drain")
                R.on("vector", "tensor_scalar", xcd[:], sx[sl][:],
                     cw_sb[:, j * 4 + 3:j * 4 + 4], None, OP.mult)
                R.on("vector", "drain")
                for kk in (2, 1, 0):
                    off = 3 - kk
                    h = R.on("vector", "scalar_tensor_tensor", xcd[:, off:T],
                             sx[sl][:, 0:T - off], cw_sb[:, j * 4 + kk:j * 4 + kk + 1],
                             xcd[:, off:T], OP.mult, OP.add)
                    if kk != 0:
                        R.on("vector", "drain")
                h.then_inc(DV.s, 1)
                dv_xcd = DV.bump()
                R.on("vector", "drain")
                state["sx_guard"][sl] = (DV.s, dv_xcd)

                pe_rep = [None] * NST
                ac_dA = [None] * NST
                dv_scan = [None] * NST
                dv_y = [None] * NST
                for step in range(NST + 2):
                    n = step
                    if n < NST:
                        gwait("tensor", state["pb_guard"])
                        if j == 0 and n == 0:
                            w("tensor", DV, state["rows_dv"])
                        for q in range(4):
                            h = R.on("tensor", "matmul", pB[:, q * 512:(q + 1) * 512],
                                 sel_sb[:, n * 128:(n + 1) * 128],
                                 dbr[:, q * 512:(q + 1) * 512],
                                 start=True, stop=True)
                        h.then_inc(PE.s, 1)
                        pe_rep[n] = PE.bump()
                        w("scalar", PE, pe_rep[n])
                        gwait("scalar", state["rep_guard"][n % 2])
                        R.on("scalar", "activation", repsb[n % 2][:], pB[:],
                             AF.Identity).then_inc(AC.s, 1)
                        ac_rep = AC.bump()
                        state["pb_guard"] = (AC.s, ac_rep)
                        gwait("scalar", state["dA_guard"][n % 2])
                        if j == 0 and n == 0:
                            w("scalar", DV, state["dreps_dv"])
                            w("scalar", DV, dv_aneg)
                        R.on("scalar", "activation", dAr[n % 2][:], dreps[:], AF.Exp,
                             scale=A_sb[:, j * NST + n: j * NST + n + 1]).then_inc(AC.s, 1)
                        ac_dA[n] = AC.bump()
                        w("vector", AC, ac_rep)
                        if n == 0:
                            w("vector", DV, dv_xcd)
                        R.on("vector", "tensor_tensor", Ur[n % 2][:], xcd[:],
                             repsb[n % 2][:], OP.mult).then_inc(DV.s, 1)
                        dv_u = DV.bump()
                        R.on("vector", "drain")
                        state["rep_guard"][n % 2] = (DV.s, dv_u)
                    m = step - 1
                    if 0 <= m < NST:
                        w("vector", AC, ac_dA[m])
                        R.on("vector", "tensor_tensor_scan", hr[m % 2][:], dAr[m % 2][:],
                             Ur[m % 2][:], 0.0, OP.mult, OP.add).then_inc(DV.s, 1)
                        dv_scan[m] = DV.bump()
                        R.on("vector", "drain")
                        state["dA_guard"][m % 2] = (DV.s, dv_scan[m])
                    p2 = step - 2
                    if 0 <= p2 < NST:
                        if p2 == 0:
                            gwait("vector", state["ysb_guard"])
                            h = R.on("vector", "tensor_scalar", ysb[:], hr[p2 % 2][:, HALF:T],
                                 crow_sb[:, j * NST + p2: j * NST + p2 + 1], None, OP.mult)
                        else:
                            h = R.on("vector", "scalar_tensor_tensor", ysb[:],
                                 hr[p2 % 2][:, HALF:T],
                                 crow_sb[:, j * NST + p2: j * NST + p2 + 1], ysb[:],
                                 OP.mult, OP.add)
                        h.then_inc(DV.s, 1)
                        dv_y[p2] = DV.bump()
                        R.on("vector", "drain")
                state["scan_done"][j] = dv_y[NST - 1]

            def emit_gate(j, ac_v):
                sl = j % 4
                w("vector", AC, ac_v)
                w("vector", DV, state["scan_done"][j])
                gwait("vector", state["o1h_guard"])
                R.on("vector", "drain")
                R.on("vector", "tensor_tensor", o1h[:], ysb[:], swv[sl][:],
                     OP.mult).then_inc(DV.s, 1)
                dv_gate = DV.bump()
                state["ysb_guard"] = (DV.s, dv_gate)
                state["swv_guard"][sl] = (DV.s, dv_gate)
                dv_last = None
                for q in range(2):
                    bi = pa_next()
                    gwait("tensor", paguard[bi])
                    w("tensor", DV, dv_gate)
                    for i in range(4):
                        ttk = q * 4 + i
                        h = R.on("tensor", "transpose",
                             pA[:, bi * 512 + i * 128: bi * 512 + (i + 1) * 128],
                             o1h[:, ttk * 128:(ttk + 1) * 128], identf_sb[:])
                    h.then_inc(PE.s, 1)
                    pe_t = PE.bump()
                    if q == 0:
                        gwait("vector", state["o1t_guard"])
                    w("vector", PE, pe_t)
                    R.on("vector", "tensor_copy", o1t[:, q * 512:(q + 1) * 512],
                         pA[:, bi * 512:(bi + 1) * 512]).then_inc(DV.s, 1)
                    dv_last = DV.bump()
                    paguard[bi] = (DV.s, dv_last)
                state["o1h_guard"] = (DV.s, dv_last)
                w("sync", DV, dv_last)
                R.on("sync", "dma_start",
                     o1s.ap()[:, j * 128:(j + 1) * 128].rearrange("(tt p) c -> p tt c", p=128),
                     o1t[:].rearrange("p (tt c) -> p tt c", c=128)).then_inc(O1T.s, 16)
                state["o1t_guard"] = (O1T.s, O1T.bump(16))

            for s in range(NFT):
                ac_last = emit_strip(s)
                wm_guard[s % 4] = (PE.s, state["strip_pe"][s])
                if s + 4 < NFT:
                    emit_strip_dma(s + 4)
                if s == 0:
                    emit_f0rows(ac_last)
                elif s % 2 == 1:
                    emit_scanblock((s - 1) // 2, ac_last)
                else:
                    emit_gate(s // 2 - 1, ac_last)
            g_o1_all = (O1T.s, O1T.n)

            bi_b = pa_next()
            gb_b = paguard[bi_b]
            def _pe_bar():
                gwait("tensor", gb_b)
                return R.on("tensor", "transpose",
                            pAb[:, bi_b * 1024: bi_b * 1024 + 128],
                            identb_sb[:], identb_sb[:])
            barrier(
                lambda: R.on("vector", "tensor_copy", stat[:, 7:8], stat[:, 0:1]),
                lambda: R.on("scalar", "activation", stat[:, 6:7], stat[:, 0:1],
                             AF.Identity),
                _pe_bar,
            )
            R.play_all(nc)

        with ExitStack() as fctx:
            ef = fctx.enter_context
            x2nT = ef(nc.sbuf_tensor("x2nT", [128, KT * HALF], BF16))
            sg = ef(nc.sbuf_tensor("sg", [128, MT * CAPN], BF16))
            hti = ef(nc.sbuf_tensor("hti", [128, MT * CAPN], BF16))
            wr = [ef(nc.sbuf_tensor(f"wr{i}", [128, 4 * FH], BF16)) for i in range(4)]
            w2r = [ef(nc.sbuf_tensor(f"w2r{i}", [128, MT * 512], BF16)) for i in range(2)]
            xsg = [ef(nc.sbuf_tensor(f"xsg_{c}", [128, D], BF16)) for c in range(3)]
            xsT = ef(nc.sbuf_tensor("xsT", [128, KT * CAPN], BF16))
            ysb2 = [ef(nc.sbuf_tensor(f"ysb2{i}", [128, D], F32)) for i in range(6)]
            prow = [ef(nc.sbuf_tensor(f"prow{i}", [128, 8], F32)) for i in range(2)]
            idxI3 = [ef(nc.sbuf_tensor(f"idxI3{i}", [128, 4], I32)) for i in range(2)]
            W8T = ef(nc.sbuf_tensor("W8T", [32, 1024], F32))
            maskT = ef(nc.sbuf_tensor("maskT", [32, 1024], F32))
            onesT = ef(nc.sbuf_tensor("onesT", [32, 1024], F32))
            posTok = ef(nc.sbuf_tensor("posTok", [128, 8 * E], F32))
            eoff_sb = ef(nc.sbuf_tensor("eoff_sb", [128, 8 * E], F32))
            tokid_sb = ef(nc.sbuf_tensor("tokid_sb", [128, 8], F32))
            vrec = ef(nc.sbuf_tensor("vrec", [128, 8 * E * 2], F32))
            idxI = ef(nc.sbuf_tensor("idxI", [128, 8 * E], I32))
            Lsb = ef(nc.sbuf_tensor("Lsb", [128, 8 * E], F32))
            Wk = ef(nc.sbuf_tensor("Wk", [128, 8 * E], F32))
            Wwork = ef(nc.sbuf_tensor("Wwork", [128, 8 * E], F32))
            Wtmp = ef(nc.sbuf_tensor("Wtmp", [128, 8 * E], F32))
            Wsmall = ef(nc.sbuf_tensor("Wsmall", [128, 8], F32))
            Wrm = ef(nc.sbuf_tensor("Wrm", [128, 8], F32))
            xs2 = [ef(nc.sbuf_tensor(f"xs2{i}", [128, D], F32)) for i in range(2)]
            _os2 = ef(nc.sbuf_tensor("os2s", [128, D], F32))
            os2 = [_os2, _os2]
            _lnb2 = ef(nc.sbuf_tensor("lnb2s", [128, D], BF16))
            lnb2 = [_lnb2, _lnb2]
            stat2 = ef(nc.sbuf_tensor("stat2", [128, 8], F32))

            paguard2 = [None] * 4
            pa2_i = [0]

            def pa2_next():
                i = pa2_i[0] % 4
                pa2_i[0] += 1
                return i

            xs_guard = [None, None]
            os_guard = [None, None]
            lnb2_guard = [None, None]
            lnb2_st_guard = [None, None]
            prev_var_dv2 = None
            ac_sq2_last = [0]
            for tt in range(8):
                sl = tt % 2
                gwait("sync", xs_guard[sl])
                R.on("sync", "dma_start", xs2[sl][:],
                     xb.ap()[HALF + tt * 128: HALF + (tt + 1) * 128, :]).then_inc(SL[sl].s, 16)
                g_xin = (SL[sl].s, SL[sl].bump(16))
                gwait("sync", os_guard[sl])
                gwait("sync", g_o1_all)
                R.on("sync", "dma_start", os2[sl][:],
                     o1s.ap()[tt * 128:(tt + 1) * 128, :]).then_inc(OS[sl].s, 16)
                g_oin = (OS[sl].s, OS[sl].bump(16))
                gwait("vector", g_xin)
                gwait("vector", g_oin)
                R.on("vector", "tensor_tensor", xs2[sl][:], xs2[sl][:], os2[sl][:],
                     OP.add).then_inc(DV.s, 1)
                dv_x2 = DV.bump()
                R.on("vector", "drain")
                os_guard[0] = os_guard[1] = (DV.s, dv_x2)
                w("sync", DV, dv_x2)
                R.on("sync", "dma_start", x2s.ap()[tt * 128:(tt + 1) * 128, :],
                     xs2[sl][:]).then_inc(SL[sl].s, 16)
                SL[sl].bump(16)
                if debug:
                    R.on("sync", "dma_start", dx2.ap()[tt * 128:(tt + 1) * 128, :],
                         xs2[sl][:]).then_inc(SL[sl].s, 16)
                    SL[sl].bump(16)
                g_x2w = (SL[sl].s, SL[sl].n)
                w("scalar", DV, dv_x2)
                if prev_var_dv2 is not None:
                    w("scalar", DV, prev_var_dv2)
                R.on("scalar", "drain")
                R.on("scalar", "activation", pB[:], xs2[sl][:], AF.Square,
                     accum_out=stat2[:, 1:2]).then_inc(AC.s, 1)
                ac_sq = AC.bump()
                ac_sq2_last[0] = ac_sq
                R.on("vector", "tensor_reduce", stat2[:, 0:1], xs2[sl][:], AX.X, OP.add)
                R.on("vector", "drain")
                R.on("vector", "tensor_scalar_mul", stat2[:, 2:3], stat2[:, 0:1], -1.0 / D)
                w("vector", AC, ac_sq)
                R.on("vector", "tensor_scalar", stat2[:, 3:4], stat2[:, 1:2], 1.0 / D,
                     float(EPS), OP.mult, OP.add)
                R.on("vector", "drain")
                R.on("vector", "tensor_tensor", stat2[:, 4:5], stat2[:, 2:3], stat2[:, 2:3], OP.mult)
                R.on("vector", "drain")
                R.on("vector", "tensor_tensor", stat2[:, 4:5], stat2[:, 3:4], stat2[:, 4:5],
                     OP.subtract).then_inc(DV.s, 1)
                dv_var = DV.bump()
                prev_var_dv2 = dv_var
                w("scalar", DV, dv_var)
                R.on("scalar", "drain")
                R.on("scalar", "activation", stat2[:, 5:6], stat2[:, 4:5],
                     AF.Sqrt).then_inc(AC.s, 1)
                ac_sd = AC.bump()
                w("vector", AC, ac_sd)
                R.on("vector", "reciprocal", stat2[:, 6:7], stat2[:, 5:6])
                R.on("vector", "drain")
                gwait("vector", lnb2_guard[sl])
                gwait("vector", lnb2_st_guard[sl])
                R.on("vector", "tensor_scalar", lnb2[sl][:], xs2[sl][:], stat2[:, 2:3],
                     stat2[:, 6:7], OP.add, OP.mult).then_inc(DV.s, 1)
                dv_ln = DV.bump()
                w("sync", DV, dv_ln)
                R.on("sync", "dma_start", x2nb.ap()[tt * 128:(tt + 1) * 128, :],
                     lnb2[sl][:]).then_inc(SL[sl].s, 16)
                lnb2_st_guard[0] = lnb2_st_guard[1] = (SL[sl].s, SL[sl].bump(16))
                xs_guard[sl] = [(DV.s, dv_ln), (AC.s, ac_sq), (SL[sl].s, SL[sl].n)]
                pe_last = None
                for q in range(2):
                    bi = pa2_next()
                    gwait("tensor", paguard2[bi])
                    w("tensor", DV, dv_ln)
                    for i in range(8):
                        j = q * 8 + i
                        h = R.on("tensor", "transpose",
                             pAb[:, bi * 1024 + i * 128: bi * 1024 + (i + 1) * 128],
                             lnb2[sl][:, j * 128:(j + 1) * 128], identb_sb[:])
                    h.then_inc(PE.s, 1)
                    pe_t = PE.bump()
                    pe_last = pe_t
                    w("vector", PE, pe_t)
                    R.on("vector", "tensor_copy",
                         x2nT[:].rearrange("p (j t) -> p j t", t=HALF)
                             [:, q * 8:(q + 1) * 8, tt * 128:(tt + 1) * 128],
                         pAb[:, bi * 1024:(bi + 1) * 1024]
                             .rearrange("p (i c) -> p i c", c=128)).then_inc(DV.s, 1)
                    dv_e = DV.bump()
                    paguard2[bi] = (DV.s, dv_e)
                lnb2_guard[0] = lnb2_guard[1] = (PE.s, pe_last)
            dv_x2nT = DV.n

            lg_guard = None
            for g8 in range(8):
                w("tensor", DV, dv_x2nT)
                w("tensor", AC, ac_sq2_last[0])
                gwait("tensor", g_wgt)
                gwait("tensor", lg_guard)
                for k in range(KT):
                    h = R.on("tensor", "matmul", pB[:, 1536:1536 + E],
                         x2nT[:, k * HALF + g8 * 128: k * HALF + (g8 + 1) * 128],
                         wgt_sb[:, k * E:(k + 1) * E],
                         start=(k == 0), stop=(k == KT - 1))
                h.then_inc(PE.s, 1)
                pe_l = PE.bump()
                w("vector", PE, pe_l)
                R.on("vector", "tensor_copy", Lsb[:, g8 * E:(g8 + 1) * E],
                     pB[:, 1536:1536 + E]).then_inc(DV.s, 1)
                lg_guard = (DV.s, DV.bump())
            L3 = Lsb[:].rearrange("p (g e) -> p g e", e=E)
            W3 = Wk[:].rearrange("p (g e) -> p g e", e=E)
            wk3 = Wwork[:].rearrange("p (g e) -> p g e", e=E)
            tq3 = Wtmp[:].rearrange("p (g e) -> p g e", e=E)
            gwait("vector", lg_guard)
            R.on("vector", "tensor_copy", Wwork[:], Lsb[:])
            R.on("vector", "memset", Wk[:], 0.0)
            R.on("vector", "drain")
            for it in range(8):
                R.on("vector", "tensor_reduce", Wsmall[:], wk3, AX.X, OP.max)
                R.on("vector", "drain")
                if it == 0:
                    R.on("vector", "tensor_copy", Wrm[:], Wsmall[:])
                    R.on("vector", "drain")
                R.on("vector", "tensor_tensor", tq3, wk3,
                     Wsmall[:].broadcast_to([128, 8, E]), OP.is_equal)
                R.on("vector", "drain")
                R.on("vector", "tensor_tensor", W3, W3, tq3, OP.max)
                R.on("vector", "scalar_tensor_tensor", wk3, tq3, -1e30, wk3,
                     OP.mult, OP.add)
                R.on("vector", "drain")
            R.on("vector", "tensor_tensor", tq3, L3,
                 Wrm[:].broadcast_to([128, 8, E]), OP.subtract).then_inc(DV.s, 1)
            dv_sub = DV.bump()
            w("scalar", DV, dv_sub)
            R.on("scalar", "drain")
            R.on("scalar", "activation", Wtmp[:], Wtmp[:], AF.Exp).then_inc(AC.s, 1)
            ac_exp = AC.bump()
            w("vector", AC, ac_exp)
            R.on("vector", "tensor_tensor", W3, tq3, W3, OP.mult)
            R.on("vector", "drain")
            R.on("vector", "tensor_reduce", Wsmall[:], W3, AX.X, OP.add)
            R.on("vector", "drain")
            R.on("vector", "reciprocal", Wrm[:], Wsmall[:])
            R.on("vector", "drain")
            R.on("vector", "tensor_tensor", W3, W3,
                 Wrm[:].broadcast_to([128, 8, E]), OP.mult).then_inc(DV.s, 1)
            dv_w8 = DV.bump()
            if debug:
                w("sync", DV, dv_w8)
                R.on("sync", "dma_start", dw8.ap(), Wk[:]).then_inc(X2S.s, 16)
                X2S.bump(16)

            rb_tok = nc.gpsimd.alloc_register()
            rb_tbl = nc.gpsimd.alloc_register()
            R.on("gpsimd", "reg_mov", rb_tok, HALF - 1)
            R.on("gpsimd", "reg_mov", rb_tbl, E * CAPN - 1)
            for dst2, src2 in [(eoff_sb[:], eoffr.ap()), (tokid_sb[:], tokidr.ap())]:
                R.on("sync", "dma_start", dst2, src2).then_inc(CONST.s, 16)
                CONST.bump(16)
            R.on("sync", "dma_start", tblD.ap(), tbl0r.ap()).then_inc(CONST.s, 16)
            g_tbl0 = (CONST.s, CONST.bump(16))
            g_consts2 = (CONST.s, CONST.n)

            R.on("vector", "memset", ysb2[0][:], 0.0).then_inc(DV.s, 1)
            dv_z = DV.bump()
            w("sync", DV, dv_z)
            for tt in range(8):
                R.on("sync", "dma_start", accH.ap()[tt * 128:(tt + 1) * 128, :],
                     ysb2[0][:]).then_inc(X2S.s, 16)
                X2S.bump(16)
            g_accz = (X2S.s, X2S.n)

            bk3_guard = {"A": None, "B": None}
            for q in range(2):
                key = "A" if q == 0 else "B"
                pt3 = pA if q == 0 else pB
                gwait("tensor", bk3_guard[key])
                w("tensor", DV, dv_w8)
                for i in range(4):
                    g8i = q * 4 + i
                    h = R.on("tensor", "transpose",
                             pt3[0:32, 1536 + i * 128:1536 + (i + 1) * 128],
                             Wk[:, g8i * 32:(g8i + 1) * 32], identf_sb[:])
                h.then_inc(PE.s, 1)
                pe_t = PE.bump()
                w("vector", PE, pe_t)
                R.on("vector", "tensor_copy",
                     W8T[:, q * 512:(q + 1) * 512].rearrange("p (i c) -> p i c", c=128),
                     pt3[0:32, 1536:2048].rearrange("p (i c) -> p i c", c=128)).then_inc(DV.s, 1)
                bk3_guard[key] = (DV.s, DV.bump())
            gwait("vector", bk3_guard["B"])
            R.on("vector", "memset", onesT[:], 1.0)
            R.on("vector", "tensor_scalar", maskT[:], W8T[:], 0.0, None, OP.is_gt)
            R.on("vector", "drain")
            R.on("vector", "tensor_tensor_scan", W8T[:], onesT[:], maskT[:], 0.0,
                 OP.mult, OP.add).then_inc(DV.s, 1)
            dv_pos = DV.bump()
            R.on("vector", "drain")
            gwait("tensor", bk3_guard["A"])
            w("tensor", DV, dv_pos)
            for g8i in range(8):
                h = R.on("tensor", "transpose",
                         pA[:, 1536 + g8i * 32:1536 + (g8i + 1) * 32],
                         W8T[:, g8i * 128:(g8i + 1) * 128], identf_sb[0:32, 0:32])
            h.then_inc(PE.s, 1)
            pe_pt = PE.bump()
            w("vector", PE, pe_pt)
            R.on("vector", "tensor_copy", posTok[:], pA[:, 1536:1536 + 256]).then_inc(DV.s, 1)
            dv_ptok = DV.bump()
            bk3_guard["A"] = (DV.s, dv_ptok)
            R.on("vector", "drain")
            gwait("vector", g_consts2)
            R.on("vector", "tensor_tensor", Wwork[:], posTok[:], eoff_sb[:], OP.add)
            R.on("vector", "drain")
            R.on("vector", "tensor_scalar", Wtmp[:], Wk[:], 0.0, None, OP.is_le)
            R.on("vector", "drain")
            R.on("vector", "scalar_tensor_tensor", Wwork[:], Wtmp[:], BIGF, Wwork[:],
                 OP.mult, OP.add)
            R.on("vector", "drain")
            R.on("vector", "tensor_scalar", Wtmp[:], posTok[:], float(CAPN), None, OP.is_gt)
            R.on("vector", "drain")
            R.on("vector", "scalar_tensor_tensor", Wwork[:], Wtmp[:], BIGF, Wwork[:],
                 OP.mult, OP.add)
            R.on("vector", "drain")
            R.on("vector", "tensor_copy", idxI[:], Wwork[:])
            R.on("vector", "drain")
            v4 = vrec[:].rearrange("p (g e two) -> p g e two", e=E, two=2)
            R.on("vector", "tensor_copy", v4[:, :, :, 0:1],
                 tokid_sb[:].rearrange("p g -> p g () ()").broadcast_to([128, 8, E, 1])
                 if False else tokid_sb[:].broadcast_to([128, 8, E, 1]))
            R.on("vector", "tensor_copy", v4[:, :, :, 1:2],
                 Wk[:].rearrange("p (g e two) -> p g e two", e=E, two=1)).then_inc(DV.s, 1)
            dv_vrec = DV.bump()
            R.on("vector", "drain")

            i4 = idxI[:].rearrange("p (g e) -> p g e", e=E)
            ts_done = [None] * E

            def emit_scatters(e):
                r = e % 4
                w("gpsimd", DV, dv_vrec)
                gwait("gpsimd", g_tbl0)
                for g8i in range(8):
                    R.on("gpsimd", "indirect_dma_start",
                         tblD.ap(), IndirectOffsetOnAxis(i4[:, g8i, e:e + 1], 0),
                         v4[:, g8i, e, :], None,
                         bounds_check=rb_tbl, oob_is_err=False).then_inc(TS[r].s, 16)
                    TS[r].bump(16)
                ts_done[e] = (TS[r].s, TS[r].n)

            for e0 in range(4):
                emit_scatters(e0)

            wr_guard = [None] * 4
            w2_guard = [None, None]
            w2_arr = [None, None]
            wrc = [0]

            def load_wchunks(mat, e):
                arr = []
                for c in range(4):
                    slid = wrc[0] % 4
                    wrc[0] += 1
                    gwait("gpsimd", wr_guard[slid])
                    R.on("gpsimd", "dma_start",
                         wr[slid][:].rearrange("p (c m) -> p c m", m=FH),
                         mat.ap()[e, :, c * 4:(c + 1) * 4, :]).then_inc(WR[slid].s, 16)
                    arr.append((slid, (WR[slid].s, WR[slid].bump(16))))
                return arr

            def gb(mt):
                return (pA if mt < 3 else pB)[:, (mt % 3) * 512:(mt % 3) * 512 + CAPN]

            g_x2nb = [(SL[0].s, SL[0].n), (SL[1].s, SL[1].n)]
            sg_guard = None
            hti_guard = None
            prow_guard = [None, None]
            idx_guard = [None, None]
            xsg_guard = None
            xsT_guard = None
            ysb_guard = [g_accz] + [None] * 5
            gu_bank_guard = None
            first_g = [True]

            def load_stage(e):
                slp = e % 2
                gwait("sync", ts_done[e])
                gwait("sync", prow_guard[slp])
                for ct in range(3):
                    R.on("sync", "dma_start", prow[slp][:, ct * 2:ct * 2 + 2],
                         tblD.ap()[e * CAPN + ct * 128: e * CAPN + (ct + 1) * 128, :]
                         ).then_inc(PL[slp].s, 16)
                    PL[slp].bump(16)
                g_pl = (PL[slp].s, PL[slp].n)
                gwait("vector", g_pl)
                gwait("vector", idx_guard[slp])
                R.on("vector", "tensor_copy", idxI3[slp][:, 0:3],
                     prow[slp][:].rearrange("p (c two) -> p c two", two=2)[:, 0:3, 0]
                     ).then_inc(DV.s, 1)
                dv_idx = DV.bump()
                R.on("vector", "drain")
                w("gpsimd", DV, dv_idx)
                gwait("gpsimd", xsg_guard)
                gwait("gpsimd", g_x2nb[0])
                gwait("gpsimd", g_x2nb[1])
                for ct in range(3):
                    R.on("gpsimd", "indirect_dma_start",
                         xsg[ct][:], None,
                         x2nb.ap(), IndirectOffsetOnAxis(idxI3[slp][:, ct:ct + 1], 0),
                         bounds_check=rb_tok, oob_is_err=False).then_inc(GA[e % 2].s, 16)
                    GA[e % 2].bump(16)
                g_ga = (GA[e % 2].s, GA[e % 2].n)
                w1a = load_wchunks(w1h, e)
                w2pre = [None, None]
                for dc in range(2):
                    slid = dc % 2
                    gwait("gpsimd", w2_guard[slid])
                    R.on("gpsimd", "dma_start",
                         w2r[slid][:].rearrange("p (k c) -> p k c", c=512),
                         w2h.ap()[e, dc]).then_inc(W2R[slid].s, 16)
                    w2pre[dc] = (W2R[slid].s, W2R[slid].bump(16))
                return dict(slp=slp, g_ga=g_ga, w1a=w1a, w2pre=w2pre)

            def compute_stage(e, LD):
                nonlocal sg_guard, hti_guard, xsg_guard, xsT_guard, gu_bank_guard
                slp = LD["slp"]
                g_ga = LD["g_ga"]
                dv_tev = None
                for ct in range(3):
                    for q in range(2):
                        key = "A" if (ct * 2 + q) % 2 == 0 else "B"
                        ptb = pAb if key == "A" else pBb
                        gwait("tensor", bk3_guard[key])
                        gwait("tensor", g_ga)
                        for i in range(8):
                            k = q * 8 + i
                            h = R.on("tensor", "transpose",
                                     ptb[:, 3072 + i * 128:3072 + (i + 1) * 128],
                                     xsg[ct][:, k * 128:(k + 1) * 128],
                                     identb_sb[:])
                        h.then_inc(PE.s, 1)
                        pe_t = PE.bump()
                        w("vector", PE, pe_t)
                        if ct == 0 and q == 0:
                            gwait("vector", xsT_guard)
                        dst = xsT[:].rearrange("p (k c) -> p k c", c=CAPN)[
                            :, q * 8:(q + 1) * 8, ct * 128:(ct + 1) * 128]
                        R.on("vector", "tensor_copy",
                             dst,
                             ptb[:, 3072:4096].rearrange("p (i c) -> p i c", c=128)
                             ).then_inc(DV.s, 1)
                        dv_tev = DV.bump()
                        bk3_guard[key] = (DV.s, dv_tev)
                dv_xsT = dv_tev
                xsg_guard = (DV.s, dv_xsT)

                w1a = LD["w1a"]
                gwait("tensor", gu_bank_guard)
                w("tensor", DV, dv_xsT)
                if first_g[0]:
                    gwait("tensor", lg_guard)
                    first_g[0] = False
                pe_gc = []
                for k in range(KT):
                    slid, arr = w1a[k // 4]
                    if k % 4 == 0:
                        gwait("tensor", arr)
                    for mt in range(MT):
                        h = R.on("tensor", "matmul", gb(mt),
                                 wr[slid][:, (k % 4) * FH + mt * 128:(k % 4) * FH + (mt + 1) * 128],
                                 xsT[:, k * CAPN:(k + 1) * CAPN],
                                 start=(k == 0), stop=(k == KT - 1))
                        if mt == MT - 1 and k % 4 == 3:
                            h.then_inc(PE.s, 1)
                    if k % 4 == 3:
                        pe_gc.append(PE.bump())
                for c in range(4):
                    wr_guard[w1a[c][0]] = (PE.s, pe_gc[c])
                pe_g = pe_gc[-1]
                gwait("scalar", sg_guard)
                w("scalar", PE, pe_g)
                for mt in range(MT):
                    h = R.on("scalar", "activation", sg[:, mt * CAPN:(mt + 1) * CAPN],
                             gb(mt), AF.Silu)
                h.then_inc(AC.s, 1)
                ac_sg = AC.bump()
                v1a = load_wchunks(v1h, e)
                w("tensor", AC, ac_sg)
                pe_uc = []
                for k in range(KT):
                    slid, arr = v1a[k // 4]
                    if k % 4 == 0:
                        gwait("tensor", arr)
                    for mt in range(MT):
                        h = R.on("tensor", "matmul", gb(mt),
                                 wr[slid][:, (k % 4) * FH + mt * 128:(k % 4) * FH + (mt + 1) * 128],
                                 xsT[:, k * CAPN:(k + 1) * CAPN],
                                 start=(k == 0), stop=(k == KT - 1))
                        if mt == MT - 1 and k % 4 == 3:
                            h.then_inc(PE.s, 1)
                    if k % 4 == 3:
                        pe_uc.append(PE.bump())
                for c in range(4):
                    wr_guard[v1a[c][0]] = (PE.s, pe_uc[c])
                pe_u = pe_uc[-1]
                gwait("vector", hti_guard)
                w("vector", PE, pe_u)
                w("vector", AC, ac_sg)
                for mt in range(MT):
                    h = R.on("vector", "tensor_tensor", hti[:, mt * CAPN:(mt + 1) * CAPN],
                             sg[:, mt * CAPN:(mt + 1) * CAPN], gb(mt), OP.mult)
                h.then_inc(DV.s, 1)
                dv_h = DV.bump()
                R.on("vector", "drain")
                sg_guard = (DV.s, dv_h)
                gu_bank_guard = (DV.s, dv_h)
                xsT_guard = (PE.s, pe_u)

                dv_ye = None
                pe_y = None
                for dc in range(4):
                    slid = dc % 2
                    if dc < 2:
                        w2_arr[slid] = LD["w2pre"][dc]
                    else:
                        gwait("gpsimd", w2_guard[slid])
                        R.on("gpsimd", "dma_start",
                             w2r[slid][:].rearrange("p (k c) -> p k c", c=512),
                             w2h.ap()[e, dc]).then_inc(W2R[slid].s, 16)
                        w2_arr[slid] = (W2R[slid].s, W2R[slid].bump(16))
                    for ct in range(3):
                        key = "A" if (dc * 3 + ct) % 2 == 0 else "B"
                        pt3 = pA if key == "A" else pB
                        w("tensor", DV, dv_h)
                        gwait("tensor", w2_arr[slid])
                        gwait("tensor", bk3_guard[key])
                        for k in range(MT):
                            h = R.on("tensor", "matmul", pt3[:, 1536:2048],
                                     hti[:, k * CAPN + ct * 128: k * CAPN + (ct + 1) * 128],
                                     w2r[slid][:, k * 512:(k + 1) * 512],
                                     start=(k == 0), stop=(k == MT - 1))
                        h.then_inc(PE.s, 1)
                        pe_y = PE.bump()
                        if ct == 2:
                            w2_guard[slid] = (PE.s, pe_y)
                        w("vector", PE, pe_y)
                        if dc == 0:
                            gwait("vector", ysb_guard[slp * 3 + ct])
                        h = R.on("vector", "tensor_scalar",
                                 ysb2[slp * 3 + ct][:, dc * 512:(dc + 1) * 512],
                                 pt3[:, 1536:2048],
                                 prow[slp][:, ct * 2 + 1:ct * 2 + 2], None, OP.mult)
                        h.then_inc(DV.s, 1)
                        dv_ye = DV.bump()
                        R.on("vector", "drain")
                        bk3_guard[key] = (DV.s, dv_ye)
                prow_guard[slp] = (DV.s, dv_ye)
                hti_guard = (PE.s, pe_y)
                return dv_ye

            def scatter_stage(e, dv_ye_last, g_ga_e):
                slp = e % 2
                w("gpsimd", DV, dv_ye_last)
                gwait("gpsimd", g_accz)
                gwait("gpsimd", (SA.s, SA.n))
                for ct in range(3):
                    for hh in range(2):
                        R.on("gpsimd", "indirect_dma_start",
                             accH.ap(),
                             IndirectOffsetOnAxis(idxI3[slp][:, ct:ct + 1], 0),
                             ysb2[slp * 3 + ct][:, hh * 1024:(hh + 1) * 1024], None,
                             element_offset=hh * 1024,
                             bounds_check=rb_tok, oob_is_err=False,
                             compute_op=OP.add).then_inc(SA.s, 16)
                        SA.bump(16)
                for ct in range(3):
                    ysb_guard[slp * 3 + ct] = (SA.s, SA.n)
                idx_guard[slp] = [g_ga_e, (SA.s, SA.n)]

            LD = load_stage(0)
            for e in range(E):
                g_ga_e = LD["g_ga"]
                dv_ye_last = compute_stage(e, LD)
                if e + 1 < E:
                    LD = load_stage(e + 1)
                    if e + 4 < E:
                        emit_scatters(e + 4)
                scatter_stage(e, dv_ye_last, g_ga_e)

            g_sa_all = (SA.s, SA.n)
            if debug:
                for e2 in range(E):
                    gwait("sync", ts_done[e2])
                R.on("sync", "dma_start", dtbl.ap(), tblD.ap()).then_inc(X2S.s, 16)
                X2S.bump(16)
                gwait("sync", g_sa_all)
                for tt in range(8):
                    R.on("sync", "dma_start", dacc.ap()[tt * 128:(tt + 1) * 128, :],
                         accH.ap()[tt * 128:(tt + 1) * 128, :]).then_inc(X2S.s, 16)
                    X2S.bump(16)

            for tt in range(8):
                sl = tt % 2
                gwait("sync", xs_guard[sl])
                R.on("sync", "dma_start", xs2[sl][:],
                     x2s.ap()[tt * 128:(tt + 1) * 128, :]).then_inc(SL[sl].s, 16)
                g_r = (SL[sl].s, SL[sl].bump(16))
                gwait("sync", os_guard[sl])
                gwait("sync", g_sa_all)
                R.on("sync", "dma_start", os2[sl][:],
                     accH.ap()[tt * 128:(tt + 1) * 128, :]).then_inc(OS[sl].s, 16)
                g_a = (OS[sl].s, OS[sl].bump(16))
                gwait("vector", g_r)
                gwait("vector", g_a)
                R.on("vector", "tensor_tensor", xs2[sl][:], xs2[sl][:], os2[sl][:],
                     OP.add).then_inc(DV.s, 1)
                dv_o = DV.bump()
                R.on("vector", "drain")
                os_guard[0] = os_guard[1] = (DV.s, dv_o)
                w("sync", DV, dv_o)
                R.on("sync", "dma_start",
                     out.ap()[tt * 128:(tt + 1) * 128, :],
                     xs2[sl][:]).then_inc(SL[sl].s, 16)
                xs_guard[sl] = [(SL[sl].s, SL[sl].bump(16))]

            for Cc in [CONST, SL[0], SL[1], OS[0], OS[1], O1T, X2S] + PL:
                R.on("sync", "wait_ge", Cc.s, Cc.n)
            for Cc in [WGT, SA] + WM + WR + W2R + TS + GA:
                R.on("gpsimd", "wait_ge", Cc.s, Cc.n)

            R.play_all(nc)

    return nc


def _host_prep(inputs):
    f32 = np.float32
    x = np.asarray(inputs["L0_attn_x"], f32)
    W = np.asarray(inputs["L0_attn_W_in"], f32)
    b = np.asarray(inputs["L0_attn_b_in"], f32)
    A_log = np.asarray(inputs["L0_attn_A_log"], f32)
    B_param = np.asarray(inputs["L0_attn_B_param"], f32)
    C_param = np.asarray(inputs["L0_attn_C_param"], f32)
    conv_w = np.asarray(inputs["L0_attn_conv_w"], f32)
    wg = np.asarray(inputs["L0_ffn_wg"], f32)
    w1 = np.asarray(inputs["L0_ffn_w1"], f32)
    v1 = np.asarray(inputs["L0_ffn_v1"], f32)
    w2 = np.asarray(inputs["L0_ffn_w2"], f32)

    W_cat = np.zeros((D, NFT * 128), f32)
    b_cat = np.zeros(NFT * 128, f32)
    W_cat[:, 0:16] = W[:, 2 * D:2 * D + 16]
    b_cat[0:16] = b[2 * D:2 * D + 16]
    W_cat[:, 64] = W[:, 2 * D + 16]
    b_cat[64] = b[2 * D + 16]
    for j in range(16):
        W_cat[:, (2 * j + 1) * 128:(2 * j + 2) * 128] = W[:, D + j * 128:D + (j + 1) * 128]
        b_cat[(2 * j + 1) * 128:(2 * j + 2) * 128] = b[D + j * 128:D + (j + 1) * 128]
        W_cat[:, (2 * j + 2) * 128:(2 * j + 3) * 128] = W[:, j * 128:(j + 1) * 128]
        b_cat[(2 * j + 2) * 128:(2 * j + 3) * 128] = b[j * 128:(j + 1) * 128]
    Wmh = np.ascontiguousarray(W_cat.reshape(KT, 128, NFT, 128).transpose(2, 1, 0, 3))
    bmh = np.ascontiguousarray(b_cat.reshape(NFT, 128).T)

    arw = np.ascontiguousarray(A_log.reshape(DT, 128, NST).transpose(1, 0, 2).reshape(128, DT * NST))
    crh = np.ascontiguousarray(C_param.reshape(DT, 128, NST).transpose(1, 0, 2).reshape(128, DT * NST))
    cwh = np.ascontiguousarray(conv_w[:, 0, :].reshape(DT, 128, 4).transpose(1, 0, 2).reshape(128, DT * 4))
    bp128 = np.zeros((128, 1), f32)
    bp128[0:16, 0] = B_param
    selm = np.zeros((NST, NST * 128), ml_dtypes.bfloat16)
    for n in range(NST):
        selm[n, n * 128:(n + 1) * 128] = 1.0
    idb = np.eye(128, dtype=ml_dtypes.bfloat16)
    idf = np.eye(128, dtype=f32)
    wgth = np.ascontiguousarray(wg.T.reshape(KT, 128, E).transpose(1, 0, 2).reshape(128, KT * E))
    w1hh = np.ascontiguousarray(w1.reshape(E, KT, 128, FH).transpose(0, 2, 1, 3))
    v1hh = np.ascontiguousarray(v1.reshape(E, KT, 128, FH).transpose(0, 2, 1, 3))
    w2hh = np.ascontiguousarray(w2.reshape(E, MT, 128, 4, 512).transpose(0, 3, 2, 1, 4))

    eoffm = np.zeros((128, 8 * E), f32)
    for e in range(E):
        eoffm[:, e::E] = e * CAPN - 1
    tokidm = np.zeros((128, 8), f32)
    for g in range(8):
        tokidm[:, g] = g * 128 + np.arange(128)
    tbl0 = np.zeros((E * CAPN, 2), f32)
    tbl0[:, 0] = BIGTOK

    shared = dict(Wm=Wmh, bmr=bmh, araw=arw, crw=crh, cwr=cwh, bpr=bp128,
                  selr=selm, identb=idb, identf=idf, wgtr=wgth,
                  w1h=w1hh, v1h=v1hh, w2h=w2hh,
                  eoffr=eoffm, tokidr=tokidm, tbl0r=tbl0)

    in_maps = []
    for bidx in range(4):
        for half in range(2):
            if half == 0:
                xbuf = np.concatenate([np.zeros((HALF, D), f32), x[bidx, 0:HALF]], axis=0)
                mr = np.concatenate([np.zeros((1, HALF), f32), np.ones((1, HALF), f32)], axis=1)
            else:
                xbuf = np.ascontiguousarray(x[bidx])
                mr = np.ones((1, T), f32)
            m = dict(shared)
            m["xb"] = xbuf
            m["mrow"] = mr
            in_maps.append(m)
    return in_maps


def kernel(**inputs):
    nc = build_nc(debug=False)
    in_maps = _host_prep(inputs)
    res = run_bass_kernel_spmd(nc, in_maps, core_ids=list(range(8)))
    outs = [res.results[c]["out"] for c in range(8)]
    full = np.concatenate(outs, axis=0).reshape(4, T, D)
    return full.astype(np.float32)
